# version 04
# baseline (speedup 1.0000x reference)
import contextlib

import numpy as np
import ml_dtypes

import concourse.bass as bass
import concourse.mybir as mybir
from concourse.bass_utils import run_bass_kernel_spmd

F32 = mybir.dt.float32
BF16 = mybir.dt.bfloat16
FP16 = mybir.dt.float16
FP8 = mybir.dt.float8e4
AF = mybir.ActivationFunctionType
ALU = mybir.AluOpType
BF = ml_dtypes.bfloat16
E4 = ml_dtypes.float8_e4m3

T, D, NH, DK = 4096, 1024, 16, 64
TT = 512
NT = T // TT
C = 64
NCH = TT // C
NP = 8
DC = 8
DFF = 2736
NFC = 22
FCW = [128] * 21 + [48]
H4 = 4096
H4C = 32

ENGS = ("sync", "tensor", "vector", "scalar", "gpsimd")

_PHASES = (1, 2, 3)
_P1STOP = None
_SCANPART = 'full'
_P2STOP = None
_NT = NT
_DBG = {}


def _threefry2x32(k1, k2, x0, x1):
    rot = [np.array([13, 15, 26, 6], np.uint32), np.array([17, 29, 16, 24], np.uint32)]
    ks = [np.uint32(k1), np.uint32(k2), np.uint32(k1) ^ np.uint32(k2) ^ np.uint32(0x1BD11BDA)]
    x = [x0.astype(np.uint32).copy(), x1.astype(np.uint32).copy()]
    x[0] += ks[0]
    x[1] += ks[1]

    def r_(v, d):
        d = np.uint32(d)
        return (v << d) | (v >> np.uint32(32 - d))

    def rounds(x, rs):
        for rr in rs:
            x[0] += x[1]
            x[1] = r_(x[1], rr)
            x[1] ^= x[0]
        return x

    x = rounds(x, rot[0]); x[0] += ks[1]; x[1] += ks[2] + np.uint32(1)
    x = rounds(x, rot[1]); x[0] += ks[2]; x[1] += ks[0] + np.uint32(2)
    x = rounds(x, rot[0]); x[0] += ks[0]; x[1] += ks[1] + np.uint32(3)
    x = rounds(x, rot[1]); x[0] += ks[1]; x[1] += ks[2] + np.uint32(4)
    x = rounds(x, rot[0]); x[0] += ks[2]; x[1] += ks[0] + np.uint32(5)
    return x[0], x[1]


def _dropout_maskmul(shape, p=0.8, seed=42):
    size = int(np.prod(shape))
    idx = np.arange(size, dtype=np.uint64)
    c1 = (idx >> np.uint64(32)).astype(np.uint32)
    c2 = (idx & np.uint64(0xFFFFFFFF)).astype(np.uint32)
    b1, b2 = _threefry2x32(np.uint32(0), np.uint32(seed), c1, c2)
    bits = b1 ^ b2
    fb = (bits >> np.uint32(9)) | np.uint32(0x3F800000)
    u = fb.view(np.float32) - np.float32(1.0)
    keep = (u < np.float32(p)).reshape(shape)
    return np.where(keep, np.float32(1.25), np.float32(0.0)).astype(BF)


class Sched:
    def __init__(self, nc, es, semnames):
        self.nc = nc
        self.ops = {e: [] for e in ENGS}
        self.sem = {}
        self.cnt = {}
        self._vw = [set(), set()]
        for n in semnames:
            self.sem[n] = es.enter_context(nc.semaphore(n))
            self.cnt[n] = 0

    def tok(self, sem):
        return (sem, self.cnt[sem])

    def wait(self, eng, *xs):
        for x in xs:
            if isinstance(x, tuple):
                sm, c = x
            else:
                sm, c = x, self.cnt[x]
            if c > 0:
                self.ops[eng].append(("w", sm, c))

    @staticmethod
    def _names(aps):
        out = set()
        for a in aps:
            t = getattr(a, "tensor", a)
            n = getattr(t, "name", None)
            if n is not None:
                out.add(n)
        return out

    def do(self, eng, fn, inc=None, n=1, r=(), w=()):
        if eng == "vector":
            rd = self._names(r)
            if rd & (self._vw[0] | self._vw[1]):
                self.ops[eng].append(("d",))
                self._vw = [set(), set()]
            self._vw = [self._names(w), self._vw[0]]
        if inc is not None:
            self.cnt[inc] += n
        self.ops[eng].append(("o", fn, inc, n))
        return self.tok(inc) if inc is not None else None

    def dma(self, eng, out, in_, inc):
        return self.do(eng, lambda e, o=out, i=in_: e.dma_start(out=o, in_=i), inc, 16)

    def barrier(self, toks):
        for e in ENGS:
            self.wait(e, *toks)

    def all_toks(self):
        return [self.tok(x) for x in ("ST", "SV", "SA", "DG", "DX")]

    def realize(self, block):
        def make(name):
            items = self.ops[name]

            def run(eng):
                floor = {}
                for it in items:
                    if it[0] == "w":
                        _, sm, c = it
                        if floor.get(sm, 0) >= c:
                            continue
                        floor[sm] = c
                        eng.wait_ge(self.sem[sm], c)
                    elif it[0] == "d":
                        eng.drain()
                    else:
                        _, fn, inc, n = it
                        ins = fn(eng)
                        if inc is not None:
                            ins.then_inc(self.sem[inc], n)
            return run

        for e in ENGS:
            getattr(block, e)(make(e))


def build_program():
    nc = bass.Bass("TRN2", target_bir_lowering=False, debug=False,
                   detect_race_conditions=False)
    dr = {}

    def din(name, shape, dt):
        dr[name] = nc.dram_tensor(name, shape, dt, kind="ExternalInput").ap()
        return dr[name]

    xfm = din("xfm", [D, T], F32)
    xtok = din("xtok", [T, D], F32)
    for wn in ("wq", "wk", "wv", "wg", "wo"):
        din(wn, [D, D], F32)
    din("w1a", [D, DFF], F32)
    din("w2a", [D, DFF], F32)
    din("w3a", [DFF, D], F32)
    ga_d = din("ga", [128, DC], F32)
    gf_d = din("gf", [128, DC], F32)
    din("fw1t", [D, H4], F32)
    din("fw2t", [H4, D], F32)
    fb1_d = din("fb1", [128, H4C], F32)
    fb2_d = din("fb2", [1, D], F32)
    mask_d = din("maskml", [T, D], BF16)
    cid_d = din("cid", [128, 128], BF16)
    cmask_d = din("cmask", [128, 128], BF16)
    cob_d = din("cob", [128, 128], BF16)
    cones_d = din("cones", [128, 1], BF16)
    conesr_d = din("conesr", [1, 128], BF16)
    conesrf_d = din("conesrf", [1, 128], F32)
    cstart_d = din("cstart", [128, TT], BF16)
    cbig_d = din("cbig", [128, TT], BF16)
    ceps_d = din("ceps", [1, 2], F32)

    x1s = nc.dram_tensor("x1s", [D, T], BF16, kind="Internal").ap()
    xq8s = nc.dram_tensor("xq8s", [D, T], FP8, kind="Internal").ap()
    r2s = nc.dram_tensor("r2s", [1, T], F32, kind="Internal").ap()
    out_d = nc.dram_tensor("out", [T, D], F32, kind="ExternalOutput").ap()
    dbg_d = {}
    for nm in ("dbg_q", "dbg_x1", "dbg_h", "dbg_o"):
        if _DBG.get(nm):
            dbg_d[nm] = nc.dram_tensor(nm, [D, T], BF16, kind="ExternalOutput").ap()
    if _DBG.get("dbg_rows"):
        dbg_d["dbg_rows"] = nc.dram_tensor("dbg_rows", [128, 2048], BF16,
                                           kind="ExternalOutput").ap()
    if _DBG.get("dbg_rowsf"):
        dbg_d["dbg_rowsf"] = nc.dram_tensor("dbg_rowsf", [2, 2048], F32,
                                            kind="ExternalOutput").ap()

    es = contextlib.ExitStack()

    def sb(name, shape, dt):
        return es.enter_context(nc.sbuf_tensor(name, shape, dt))

    cid = sb("cid_s", [128, 128], BF16)
    cmask = sb("cmask_s", [128, 128], BF16)
    cob = sb("cob_s", [128, 128], BF16)
    cones = sb("cones_s", [128, 1], BF16)
    conesr = sb("conesr_s", [1, 128], BF16)
    conesrf = sb("conesrf_s", [1, 128], F32)
    cstart = sb("cstart_s", [128, TT], BF16)
    cbig = sb("cbig_s", [128, TT], BF16)
    ceps = sb("ceps_s", [1, 2], F32)
    ga = sb("ga_s", [128, DC], F32)
    gf = sb("gf_s", [128, DC], F32)
    b1q = sb("b1q", [128, H4C], F32)
    b1q8 = sb("b1q8", [128, H4C], FP8)
    b2row8 = sb("b2row8", [1, D], FP8)
    b2rowb = sb("b2rowb", [1, D], BF16)
    b2tile = sb("b2tile", [128, D], BF16)
    Sst = sb("Sst", [128, 1024], BF16)
    lamC = sb("lamC", [128, 64], F32)
    rows = sb("rows", [1, 2048], F32)
    rowbf = sb("rowbf", [1, 1024], BF16)
    

    psA = es.enter_context(nc.psum_tensor("psA", [128, 512], F32))
    psB = es.enter_context(nc.psum_tensor("psB", [128, 512], F32))
    psO = es.enter_context(nc.psum_tensor("psO", [128, 512], F32))
    psK = es.enter_context(nc.psum_tensor("psK", [128, 512], F32))
    psW = es.enter_context(nc.psum_tensor("psW", [128, 1024], F32))
    psT = es.enter_context(nc.psum_tensor("psT", [128, 1024], BF16))
    psS = es.enter_context(nc.psum_tensor("psS", [128, 512], F32))

    s = Sched(nc, es, ["DX", "DG", "ST", "SV", "SA", "SG"])

    def mm(psap, lhs, rhs, start, stop, tp=None, inc=None):
        skip = getattr(psap, "offset", 0) >= 4096
        return s.do("tensor", lambda e, o=psap, l=lhs, rr=rhs, st_=start, sp=stop, t=tp,
                    sk=skip:
                    e.matmul(o, l, rr, start=st_, stop=sp, tile_position=t,
                             skip_group_check=sk), inc=inc)

    def tpose(psap, inap, inc=None):
        def f(e, o=psap, i=inap):
            with nc.allow_low_precision(reason="bf16 transpose to bf16 psum"):
                return e.transpose(o, i, cid[:])
        return s.do("tensor", f, inc=inc)

    def act(out, in_, func, bias=0.0, scale=1.0, inc="SA"):
        if isinstance(bias, float) and bias != 0.0:
            bias = ceps[0:1, 0:1] if bias == 1e-5 else ceps[0:1, 1:2]
        return s.do("scalar", lambda e, o=out, i=in_, f=func, b=bias, sc=scale:
                    e.activation(out=o, in_=i, func=f, bias=b, scale=sc), inc=inc)

    def vtt(out, in0, in1, op, inc=None, rn=(), wn=()):
        return s.do("vector", lambda e, o=out, a=in0, b=in1, p_=op:
                    e.tensor_tensor(out=o, in0=a, in1=b, op=p_), inc=inc,
                    r=list(rn) + [in0, in1], w=list(wn) + [out])

    def vrecip(out, in_, inc=None, rn=(), wn=()):
        def f(e, o=out, i=in_):
            with nc.allow_low_precision(reason="bf16 reciprocal by design"):
                return e.reciprocal(out=o, in_=i)
        return s.do("vector", f, inc=inc, r=list(rn) + [in_], w=list(wn) + [out])

    def vcp(out, in_, inc=None):
        return s.do("vector", lambda e, o=out, i=in_: e.tensor_copy(out=o, in_=i),
                    inc=inc, r=[in_], w=[out])

    for dsrc, tgt in ((cid_d, cid), (cmask_d, cmask), (cob_d, cob), (cones_d, cones),
                      (conesr_d, conesr), (conesrf_d, conesrf), (cstart_d, cstart), (cbig_d, cbig),
                      (ceps_d, ceps), (ga_d, ga), (gf_d, gf)):
        s.dma("sync", tgt[:], dsrc, "DX")
    s.dma("sync", b1q[:], fb1_d, "DX")
    s.dma("sync", rows[0:1, 1024:1024 + D], fb2_d, "DX")
    c0 = s.tok("DX")
    s.wait("scalar", c0)
    act(b1q8[:], b1q[:], AF.Copy)
    act(b1q[:], b1q8[:], AF.Copy)
    act(b2row8[:], rows[0:1, 1024:1024 + D], AF.Copy)
    tb2 = act(b2rowb[:], b2row8[:], AF.Copy)
    s.wait("tensor", tb2)
    mm(psW[0:128, 0:512], conesr[:], b2rowb[0:1, 0:512], True, True)
    t1 = mm(psW[0:128, 512:1024], conesr[:], b2rowb[0:1, 512:1024], True, True, inc="ST")
    s.wait("scalar", t1)
    b2done = act(b2tile[:], psW[0:128, 0:1024], AF.Copy)

    ph1 = contextlib.ExitStack()

    def sb1(name, shape, dt):
        return ph1.enter_context(nc.sbuf_tensor(name, shape, dt))

    wqb = sb1("wqb", [128, DC * D], BF16)
    wkb = sb1("wkb", [128, DC * D], BF16)
    wvb = sb1("wvb", [128, DC * D], BF16)
    wgb = sb1("wgb", [128, DC * D], BF16)
    wob = sb1("wob", [128, DC * D], BF16)
    wst = sb1("wst", [128, D], F32)
    wtoks = []
    wi = 0
    for wname, wtgt in (("wq", wqb), ("wk", wkb), ("wg", wgb), ("wv", wvb), ("wo", wob)):
        for c in range(DC):
            if wi >= 1:
                s.wait("sync", wtoks[wi - 1])
            ld = s.dma("sync", wst[:], dr[wname][128 * c:128 * (c + 1), :], "DX")
            if wi % 2 == 0:
                s.wait("vector", ld)
                tk = vcp(wtgt[:, D * c:D * (c + 1)], wst[:], inc="SV")
            else:
                s.wait("scalar", ld)
                tk = act(wtgt[:, D * c:D * (c + 1)], wst[:], AF.Copy)
            wtoks.append(tk)
            wi += 1
    w_ready = [s.tok("SV"), s.tok("SA")]

    xfp = sb1("xfp", [128, 2 * TT], F32)
    xbf = sb1("xbf", [128, TT * DC], BF16)
    xsq = sb1("xsq", [128, 2 * TT], BF16)
    tmp1 = sb1("tmp1", [128, 2 * TT], F32)
    xq8t = sb1("xq8t", [128, 2 * TT], FP8)
    xnb = sb1("xnb", [128, TT * DC], BF16)
    ht = sb1("ht", [128, TT * DC], BF16)
    qb = sb1("qb", [128, 2 * TT], BF16)
    kb = sb1("kb", [128, 2 * TT], BF16)
    ab = sb1("ab", [128, 2 * TT], BF16)
    iat = sb1("iat", [128, TT], BF16)
    d1a = sb1("d1a", [128, TT], BF16)
    d1b = sb1("d1b", [128, TT], BF16)
    lam = sb1("lam", [128, TT], BF16)
    ilam = sb1("ilam", [128, TT], BF16)
    sqt = sb1("sqt", [128, TT], BF16)
    dqM = sb1("dqM", [128, TT], BF16)
    dkM = sb1("dkM", [128, TT], BF16)
    khf = sb1("khf", [128, TT], BF16)
    Qt = sb1("Qt", [128, TT * NP], BF16)
    Kt = sb1("Kt", [128, TT * NP], BF16)
    Khtok = sb1("Khtok", [128, TT * DC], BF16)
    Vtok = sb1("Vtok", [128, TT * DC], BF16)
    ptl = sb1("ptl", [128, 256], BF16)
    Ofm = sb1("Ofm", [128, TT * NP], BF16)
    x1t = sb1("x1t", [128, TT * DC], BF16)

    s.do("vector", lambda e: e.memset(Sst[:], 0.0), inc="SV", w=[Sst])
    s.barrier(w_ready + [s.tok("SA")])

    mask_toks = {}
    upd_toks = {}

    def emit_cell_head(g, ch, p, ci, prev):
        par = ch % 2
        b = 64 * par
        pr = slice(b, b + 64)
        tc = slice(TT * p + C * ch, TT * p + C * (ch + 1))
        pc = 64 * (ci % 2)
        if prev is not None:
            s.wait("tensor", mask_toks[(prev[0], prev[2])])
        mm(psW[pr, pc:pc + 64], Kt[0:64, tc], Qt[0:64, tc], True, True, (0, b))
        return mm(psW[pr, 512 + pc:512 + pc + 64], Kt[64:128, tc], Qt[64:128, tc],
                  True, True, (64, b), inc="ST")

    def emit_cell_mask(g, ch, p, ci, pmm):
        par = ch % 2
        b = 64 * par
        pr = slice(b, b + 64)
        pc = 64 * (ci % 2)
        ps_ = 128 * (ci % 2)
        s.wait("vector", pmm)
        mask_toks[(g, p)] = s.do(
            "vector", lambda e, pr=pr, pc=pc, ps_=ps_: e.tensor_tensor(
                out=ptl[pr, ps_:ps_ + 128].rearrange("q (a x) -> q a x", a=2),
                in0=psW[pr, 0:1024].rearrange("q (a x) -> q a x", a=2)[:, :, pc:pc + 64],
                in1=cmask[pr, 0:128].rearrange("q (a x) -> q a x", a=2),
                op=ALU.mult),
            inc="SV", r=[psW, cmask], w=[ptl])

    def emit_cell_tail(i, g, ch, p, ci):
        if _SCANPART == "head":
            return
        par = ch % 2
        b = 64 * par
        pr = slice(b, b + 64)
        tc = slice(TT * p + C * ch, TT * p + C * (ch + 1))
        sps = 512 * (g % 2)
        spd = 512 * ((g + 1) % 2)
        vbase = 1024 * (ch // 2) + 128 * p
        pc = 64 * (ci % 2)
        ps_ = 128 * (ci % 2)
        s.wait("tensor", mask_toks[(g, p)])
        mm(psO[0:64, pc:pc + 64], Vtok[pr, vbase:vbase + 64], ptl[pr, ps_:ps_ + 64],
           True, True, (b, 0))
        mm(psO[64:128, pc:pc + 64], Vtok[pr, vbase + 64:vbase + 128],
           ptl[pr, ps_ + 64:ps_ + 128], True, True, (b, 64))
        mm(psK[0:64, pc:pc + 64], Khtok[pr, vbase:vbase + 64], Vtok[pr, vbase:vbase + 64],
           True, True, (b, 0))
        kvm = mm(psK[64:128, pc:pc + 64], Khtok[pr, vbase + 64:vbase + 128],
                 Vtok[pr, vbase + 64:vbase + 128], True, True, (b, 64), inc="ST")
        if (g - 1, p) in upd_toks:
            s.wait("tensor", upd_toks[(g - 1, p)])
        mm(psS[0:64, pc:pc + 64], Sst[0:64, sps + 64 * p:sps + 64 * (p + 1)],
           Qt[0:64, tc], True, True, (0, 0))
        qsm = mm(psS[64:128, pc:pc + 64], Sst[64:128, sps + 64 * p:sps + 64 * (p + 1)],
                 Qt[64:128, tc], True, True, (64, 64), inc="ST")
        s.wait("vector", qsm)
        s.do("vector", lambda e, tc=tc, pc=pc: e.tensor_copy(
            out=Ofm[:, tc], in_=psO[:, pc:pc + 64]), r=[psO], w=[Ofm])
        s.do("vector", lambda e, tc=tc, pc=pc: e.tensor_tensor(
            out=Ofm[:, tc], in0=Ofm[:, tc], in1=psS[:, pc:pc + 64], op=ALU.add),
            inc="SV", r=[Ofm, psS], w=[Ofm])
        s.wait("vector", kvm)
        upd_toks[(g, p)] = s.do(
            "vector", lambda e, p=p, ch=ch, sps=sps, spd=spd, pc=pc: e.scalar_tensor_tensor(
                out=Sst[:, spd + 64 * p:spd + 64 * (p + 1)],
                in0=Sst[:, sps + 64 * p:sps + 64 * (p + 1)],
                scalar=lamC[:, 8 * p + ch:8 * p + ch + 1],
                in1=psK[:, pc:pc + 64], op0=ALU.mult, op1=ALU.add),
            inc="SV", r=[Sst, lamC, psK], w=[Sst])

    def emit_p1_tile(i, prev_toks):
        t0 = i * TT
        if prev_toks is not None:
            s.wait("vector", prev_toks[0], prev_toks[2])
            s.wait("scalar", prev_toks[0])
            s.wait("sync", prev_toks[1])
            s.wait("tensor", prev_toks[1])
        xtoks = []
        cast_toks = []
        sum_toks = []
        for c in range(DC):
            pp = c % 2
            if c >= 2:
                s.wait("sync", cast_toks[c - 2])
            xtoks.append(s.dma("sync", xfp[:, TT * pp:TT * (pp + 1)],
                               xfm[128 * c:128 * (c + 1), t0:t0 + TT], "DX"))
            s.wait("scalar", xtoks[c])
            cast_toks.append(act(xbf[:, TT * c:TT * (c + 1)],
                                 xfp[:, TT * pp:TT * (pp + 1)], AF.Copy))
            if c >= 2:
                s.wait("scalar", sum_toks[c - 2])
            sqk = act(xsq[:, TT * (c % 2):TT * (c % 2 + 1)],
                      xbf[:, TT * c:TT * (c + 1)], AF.Square)
            s.wait("tensor", sqk)
            sum_toks.append(mm(psS[0:1, 0:TT], cones[:],
                               xsq[:, TT * (c % 2):TT * (c % 2 + 1)],
                               c == 0, False if c < DC - 1 else True, inc="ST"))
        sx2 = s.tok("ST")
        s.wait("vector", sx2)
        s.do("vector", lambda e: e.tensor_scalar_mul(
            out=rows[0:1, TT:2 * TT], in0=psS[0:1, 0:TT], scalar1=1.0 / D),
            inc="SV", r=[psS], w=[rows])
        ev1 = s.tok("SV")
        for c in range(DC):
            s.wait("tensor", ev1)
            mm(psS[0:1, 0:TT], cones[:], xbf[:, TT * c:TT * (c + 1)],
               c == 0, c == DC - 1, inc="ST" if c == DC - 1 else None)
        sx1 = s.tok("ST")
        s.wait("vector", sx1)
        mevtk = s.do("vector", lambda e: e.tensor_scalar_mul(
            out=rows[0:1, 0:TT], in0=psS[0:1, 0:TT], scalar1=1.0 / D),
            inc="SV", r=[psS], w=[rows])
        if "dbg_rowsf" in dbg_d and i == 0:
            s.wait("gpsimd", mevtk)
            s.dma("gpsimd", dbg_d["dbg_rowsf"][0:1, :], rows[0:1, :], "DG")
            s.wait("vector", s.tok("DG"))
        vtt(rows[0:1, 2 * TT:3 * TT], rows[0:1, 0:TT], rows[0:1, 0:TT], ALU.mult)
        vrow = vtt(rows[0:1, 2 * TT:3 * TT], rows[0:1, TT:2 * TT],
                   rows[0:1, 2 * TT:3 * TT], ALU.subtract, inc="SV")
        s.wait("scalar", vrow)
        sq1 = act(rows[0:1, 3 * TT:4 * TT], rows[0:1, 2 * TT:3 * TT], AF.Sqrt, bias=1e-5)
        s.wait("vector", sq1)
        s.do("vector", lambda e: e.reciprocal(out=rows[0:1, TT:2 * TT],
                                              in_=rows[0:1, 3 * TT:4 * TT]),
             inc="SV", r=[rows], w=[rows])
        vtt(rows[0:1, 3 * TT:4 * TT], rows[0:1, 0:TT], rows[0:1, TT:2 * TT], ALU.mult)
        vtt(rows[0:1, 0:TT], rows[0:1, 2 * TT:3 * TT], rows[0:1, TT:2 * TT], ALU.mult)
        vs2 = vtt(rows[0:1, 0:TT], rows[0:1, 0:TT], rows[0:1, TT:2 * TT], ALU.mult, inc="SV")
        s.wait("scalar", vs2)
        sq2 = act(rows[0:1, 2 * TT:3 * TT], rows[0:1, 0:TT], AF.Sqrt, bias=1e-6)
        s.wait("vector", sq2)
        s.do("vector", lambda e: e.reciprocal(out=rows[0:1, 0:TT],
                                              in_=rows[0:1, 2 * TT:3 * TT]),
             inc="SV", r=[rows], w=[rows])
        if "dbg_rowsf" in dbg_d and i == 0:
            s.wait("gpsimd", s.tok("SV"))
            s.dma("gpsimd", dbg_d["dbg_rowsf"][1:2, :], rows[0:1, :], "DG")
            s.wait("vector", s.tok("DG"))
        for src0, tgt in ((TT, psA), (3 * TT, psB), (0, psO)):
            vcp(rowbf[0:1, 0:TT], rows[0:1, src0:src0 + TT])
            vtt(rows[0:1, 2 * TT:3 * TT], rows[0:1, src0:src0 + TT],
                rowbf[0:1, 0:TT], ALU.subtract)
            hl = vcp(rowbf[0:1, TT:2 * TT], rows[0:1, 2 * TT:3 * TT], inc="SV")
            s.wait("tensor", hl)
            mm(tgt[0:128, 0:TT], conesr[:], rowbf[0:1, 0:TT], True, False)
            bm = mm(tgt[0:128, 0:TT], conesr[:], rowbf[0:1, TT:2 * TT], False, True,
                    inc="ST")
            s.wait("vector", bm)
        if "dbg_rows" in dbg_d and i == 0:
            s.wait("scalar", bm)
            act(xsq[:, 0:TT], psA[0:128, 0:TT], AF.Copy)
            act(xsq[:, TT:2 * TT], psB[0:128, 0:TT], AF.Copy)
            dr8 = act(xbf[:, 0:TT], psO[0:128, 0:TT], AF.Copy)
            s.wait("gpsimd", dr8)
            s.dma("gpsimd", dbg_d["dbg_rows"][:, 0:TT], xsq[:, 0:TT], "DG")
            s.dma("gpsimd", dbg_d["dbg_rows"][:, TT:2 * TT], xsq[:, TT:2 * TT], "DG")
            s.dma("gpsimd", dbg_d["dbg_rows"][:, 2 * TT:3 * TT], xbf[:, 0:TT], "DG")
            s.dma("gpsimd", dbg_d["dbg_rows"][0:1, 3 * TT:4 * TT], rows[0:1, TT:2 * TT], "DG")
            s.wait("vector", s.tok("DG"))
            s.wait("scalar", s.tok("DG"))
        s.wait("vector", bm)
        xn_sv = []
        q8dma = []
        xdma2 = []
        xnb_acts = []
        for c in range(DC):
            pp = c % 2
            if c >= 2:
                s.wait("sync", xn_sv[c - 2])
            else:
                s.wait("sync", cast_toks[DC - 2 + c])
            xdma2.append(s.dma("sync", xfp[:, TT * pp:TT * (pp + 1)],
                               xfm[128 * c:128 * (c + 1), t0:t0 + TT], "DX"))
            s.wait("vector", xdma2[c])
            if c >= 2:
                s.wait("vector", xnb_acts[c - 2])
            vtt(tmp1[:, TT * pp:TT * (pp + 1)], xfp[:, TT * pp:TT * (pp + 1)],
                psA[0:128, 0:TT], ALU.mult)
            x2t = vtt(tmp1[:, TT * pp:TT * (pp + 1)], tmp1[:, TT * pp:TT * (pp + 1)],
                      psB[0:128, 0:TT], ALU.subtract, inc="SV")
            s.wait("scalar", x2t)
            if 3 in _PHASES:
                if c >= 2 and q8dma[c - 2] is not None:
                    s.wait("scalar", q8dma[c - 2])
                q8 = act(xq8t[:, TT * pp:TT * (pp + 1)], tmp1[:, TT * pp:TT * (pp + 1)],
                         AF.Copy)
                s.wait("gpsimd", q8)
                q8dma.append(s.dma("gpsimd", xq8s[128 * c:128 * (c + 1), t0:t0 + TT],
                                   xq8t[:, TT * pp:TT * (pp + 1)], "DG"))
            xnb_acts.append(act(xnb[:, TT * c:TT * (c + 1)], tmp1[:, TT * pp:TT * (pp + 1)],
                                AF.Copy))
            vtt(xsq[:, TT * pp:TT * (pp + 1)], tmp1[:, TT * pp:TT * (pp + 1)],
                psO[0:128, 0:TT], ALU.mult)
            htt = s.do("vector", lambda e, c=c, pp=pp: e.tensor_scalar(
                out=ht[:, TT * c:TT * (c + 1)], in0=xsq[:, TT * pp:TT * (pp + 1)],
                scalar1=ga[:, c:c + 1], scalar2=None, op0=ALU.mult),
                inc="SV", r=[xsq, ga], w=[ht])
            xn_sv.append(htt)
        htok = s.tok("SV")
        s.wait("tensor", s.tok("SA"))
        if _P1STOP == "ln":
            return [s.tok("ST"), s.tok("SV"), s.tok("DG")]
        evac_toks = {}
        tp_toks = {}
        prep_psO = None
        s.wait("tensor", htok)
        prep_sv = {}
        for p in range(NP):
            pp = p % 2
            pban, poth = (psA, psB) if pp == 0 else (psB, psA)
            if p >= 1:
                s.wait("tensor", evac_toks[p - 1])
            if p >= 2:
                s.wait("tensor", prep_sv[p - 2])
            for c in range(DC):
                mm(pban[0:128, 0:TT], wqb[:, D * c + 128 * p:D * c + 128 * (p + 1)],
                   ht[:, TT * c:TT * (c + 1)], c == 0, c == DC - 1,
                   inc="ST" if c == DC - 1 else None)
            qmm = s.tok("ST")
            for c in range(DC):
                mm(poth[0:128, 0:TT], wkb[:, D * c + 128 * p:D * c + 128 * (p + 1)],
                   ht[:, TT * c:TT * (c + 1)], c == 0, c == DC - 1,
                   inc="ST" if c == DC - 1 else None)
            kmm = s.tok("ST")
            s.wait("scalar", qmm)
            qev = act(qb[:, TT * pp:TT * (pp + 1)], pban[0:128, 0:TT], AF.Copy)
            s.wait("scalar", kmm)
            kev = act(kb[:, TT * pp:TT * (pp + 1)], poth[0:128, 0:TT], AF.Copy)
            s.wait("tensor", qev)
            for c in range(DC):
                mm(pban[0:128, 0:TT], wgb[:, D * c + 128 * p:D * c + 128 * (p + 1)],
                   ht[:, TT * c:TT * (c + 1)], c == 0, c == DC - 1,
                   inc="ST" if c == DC - 1 else None)
            gmm = s.tok("ST")
            s.wait("scalar", gmm)
            gev = act(ab[:, TT * pp:TT * (pp + 1)], pban[0:128, 0:TT], AF.Sigmoid)
            evac_toks[p] = gev
            qsl = qb[:, TT * pp:TT * (pp + 1)]
            ksl = kb[:, TT * pp:TT * (pp + 1)]
            asl = ab[:, TT * pp:TT * (pp + 1)]
            s.wait("scalar", gev)
            sq1p = act(sqt[:], qsl, AF.Square)
            if prep_psO is not None:
                s.wait("tensor", prep_psO)
            s.wait("tensor", sq1p)
            m1 = mm(psO[0:128, 0:TT], cob[:], sqt[:], True, True, inc="ST")
            s.wait("scalar", m1)
            dq1 = act(tmp1[:, 0:TT], psO[0:128, 0:TT], AF.Sqrt)
            s.wait("vector", dq1)
            s.do("vector", lambda e: e.tensor_scalar_max(out=tmp1[:, 0:TT], in0=tmp1[:, 0:TT],
                                                         scalar1=1e-12),
                 r=[tmp1], w=[tmp1])
            vrecip(dqM[:], tmp1[:, 0:TT], inc="SV")
            dq2 = s.tok("SV")
            sq2p = act(sqt[:], ksl, AF.Square)
            s.wait("tensor", sq2p, dq2)
            m2 = mm(psO[0:128, 0:TT], cob[:], sqt[:], True, True, inc="ST")
            s.wait("scalar", m2)
            dk1 = act(tmp1[:, 0:TT], psO[0:128, 0:TT], AF.Sqrt)
            s.wait("vector", dk1)
            s.do("vector", lambda e: e.tensor_scalar_max(out=tmp1[:, 0:TT], in0=tmp1[:, 0:TT],
                                                         scalar1=1e-12),
                 r=[tmp1], w=[tmp1])
            vrecip(dkM[:], tmp1[:, 0:TT], inc="SV")
            prep_psO = s.tok("SA")
            vrecip(iat[:], asl, rn=[ab])
            vtt(d1a[:], asl, cstart[:], ALU.mult, rn=[ab])
            vtt(d1b[:], iat[:], cstart[:], ALU.mult)
            vtt(d1b[:], d1b[:], cbig[:], ALU.add)
            s.do("vector", lambda e, a=asl: e.tensor_tensor_scan(
                out=lam[:], data0=a, data1=d1a[:], initial=1.0,
                op0=ALU.mult, op1=ALU.max), r=[ab, d1a], w=[lam])
            s.do("vector", lambda e: e.tensor_tensor_scan(
                out=ilam[:], data0=iat[:], data1=d1b[:], initial=1.0,
                op0=ALU.mult, op1=ALU.min), r=[iat, d1b], w=[ilam])
            s.do("vector", lambda e, p=p: e.tensor_copy(
                out=lamC[:, 8 * p:8 * (p + 1)],
                in_=lam.rearrange("q (c x) -> q c x", x=C)[:, :, C - 1]),
                r=[lam], w=[lamC])
            vtt(sqt[:], qsl, lam[:], ALU.mult, rn=[qb])
            vtt(Qt[:, TT * p:TT * (p + 1)], sqt[:], dqM[:], ALU.mult)
            vtt(sqt[:], ksl, ilam[:], ALU.mult, rn=[kb])
            vtt(Kt[:, TT * p:TT * (p + 1)], sqt[:], dkM[:], ALU.mult)
            for ch in range(NCH):
                khdone = s.do("vector", lambda e, p=p, ch=ch: e.tensor_scalar(
                    out=khf[:, C * ch:C * (ch + 1)],
                    in0=Kt[:, TT * p + C * ch:TT * p + C * (ch + 1)],
                    scalar1=lamC[:, 8 * p + ch:8 * p + ch + 1], scalar2=None,
                    op0=ALU.mult), inc="SV" if ch == NCH - 1 else None,
                    r=[Kt, lamC], w=[khf])
            prep_sv[p] = khdone
            s.wait("tensor", khdone)
            if p > 0:
                s.wait("tensor", tp_toks[p - 1])
            tb = 512 * (p % 2)
            for sub in range(4):
                tpose(psT[:, tb + 128 * sub:tb + 128 * (sub + 1)],
                      khf[:, 128 * sub:128 * (sub + 1)],
                      inc="ST" if sub == 3 else None)
            tpmm = s.tok("ST")
            s.wait("vector", tpmm)
            tp_toks[p] = s.do("vector", lambda e, p=p, tb=tb: e.tensor_copy(
                out=Khtok.rearrange("q (sx qq x) -> q sx qq x", sx=4, qq=8)[:, :, p, :],
                in_=psT[:, tb:tb + 512].rearrange("q (sx x) -> q sx x", sx=4)),
                inc="SV", r=[psT], w=[Khtok])
        if _P1STOP == "proj":
            return [s.tok("ST"), s.tok("SV"), s.tok("DG")]
        vev = []
        vg = 0
        s.wait("tensor", evac_toks[NP - 1])
        for sub in range(4):
            for hf in range(2):
                pban = psA if vg % 2 == 0 else psB
                if vg >= 2:
                    s.wait("tensor", vev[vg - 2])
                for c in range(DC):
                    mm(pban[0:128, 0:TT],
                       ht[:, TT * c + 128 * sub:TT * c + 128 * (sub + 1)],
                       wvb[:, D * c + 512 * hf:D * c + 512 * (hf + 1)],
                       c == 0, c == DC - 1, inc="ST" if c == DC - 1 else None)
                vmm = s.tok("ST")
                s.wait("vector", vmm)
                tk = s.do("vector", lambda e, sub=sub, hf=hf, pban=pban: e.tensor_copy(
                    out=Vtok[:, 1024 * sub + 512 * hf:1024 * sub + 512 * (hf + 1)],
                    in_=pban[0:128, 0:TT]), inc="SV", r=[pban], w=[Vtok])
                vev.append(tk)
                vg += 1
        s.wait("tensor", s.tok("SV"), s.tok("SA"))
        cells = [(ch, p) for ch in range(NCH) for p in range(NP)]
        prev = None
        for ci, (ch, p) in enumerate(cells):
            g = NCH * i + ch
            pmm = emit_cell_head(g, ch, p, ci, prev)
            if prev is not None:
                pg, pch, ppp, pci = prev
                emit_cell_tail(i, pg, pch, ppp, pci)
            emit_cell_mask(g, ch, p, ci, pmm)
            prev = (g, ch, p, ci)
        if prev is not None:
            emit_cell_tail(i, prev[0], prev[1], prev[2], prev[3])
        if _P1STOP == "scan":
            return [s.tok("ST"), s.tok("SV"), s.tok("DG")]
        sv_now = s.tok("SV")
        s.wait("tensor", sv_now)
        wo_ev = []
        r2sum = []
        for oc in range(DC):
            pban = psA if oc % 2 == 0 else psB
            if oc >= 2:
                s.wait("tensor", wo_ev[oc - 2])
            for p in range(NP):
                mm(pban[0:128, 0:TT], wob[:, D * p + 128 * oc:D * p + 128 * (oc + 1)],
                   Ofm[:, TT * p:TT * (p + 1)], p == 0, p == NP - 1,
                   inc="ST" if p == NP - 1 else None)
            wmm = s.tok("ST")
            s.wait("vector", wmm)
            x1e = vtt(x1t[:, TT * oc:TT * (oc + 1)], pban[0:128, 0:TT],
                      xnb[:, TT * oc:TT * (oc + 1)], ALU.add, inc="SV")
            wo_ev.append(x1e)
            s.wait("scalar", x1e)
            if oc >= 2:
                s.wait("scalar", r2sum[oc - 2])
            xsqe = act(xsq[:, TT * (oc % 2):TT * (oc % 2 + 1)],
                       x1t[:, TT * oc:TT * (oc + 1)], AF.Square)
            s.wait("tensor", xsqe)
            r2sum.append(mm(psS[0:1, 0:TT], cones[:],
                            xsq[:, TT * (oc % 2):TT * (oc % 2 + 1)],
                            oc == 0, oc == DC - 1, inc="ST"))
            if 2 in _PHASES:
                s.wait("gpsimd", x1e)
                s.dma("gpsimd", x1s[128 * oc:128 * (oc + 1), t0:t0 + TT],
                      x1t[:, TT * oc:TT * (oc + 1)], "DG")
        r2mm = s.tok("ST")
        s.wait("scalar", r2mm)
        r2a = act(rows[0:1, 0:TT], psS[0:1, 0:TT], AF.Sqrt, bias=1e-6, scale=1.0 / D)
        s.wait("vector", r2a)
        r2v = s.do("vector", lambda e: e.reciprocal(
            out=rows[0:1, TT:2 * TT], in_=rows[0:1, 0:TT]), inc="SV",
            r=[rows], w=[rows])
        s.wait("gpsimd", r2v)
        s.dma("gpsimd", r2s[0:1, t0:t0 + TT], rows[0:1, TT:2 * TT], "DG")
        for nm, buf in (("dbg_q", Qt), ("dbg_h", ht), ("dbg_o", Ofm), ("dbg_x1", x1t)):
            if nm in dbg_d:
                dd = s.tok("SV")
                s.wait("gpsimd", dd, s.tok("SA"))
                nchunk = buf.shape[1] // TT
                for c in range(min(DC, nchunk)):
                    s.dma("gpsimd", dbg_d[nm][128 * c:128 * (c + 1), t0:t0 + TT],
                          buf[:, TT * c:TT * (c + 1)], "DG")
        return [s.tok("ST"), s.tok("SV"), s.tok("DG")]

    if 1 in _PHASES:
        prev_toks = None
        for i in range(_NT):
            prev_toks = emit_p1_tile(i, prev_toks)

    ph1_end = s.all_toks()
    ph1.close()

    if 2 in _PHASES:
        ph2 = contextlib.ExitStack()

        def sb2(name, shape, dt):
            return ph2.enter_context(nc.sbuf_tensor(name, shape, dt))

        w1b = sb2("w1b", [128, DC * DFF], BF16)
        w2b = sb2("w2b", [128, DC * DFF], BF16)
        w3b = sb2("w3b", [128, NFC * D], BF16)
        wst2 = sb2("wst2", [128, DFF // 2], F32)
        x1r = sb2("x1r", [128, TT * DC], BF16)
        h2t = sb2("h2t", [128, TT * DC], BF16)
        s1t = sb2("s1t", [128, TT], BF16)
        sgl = sb2("sgl", [128, NFC * TT], BF16)
        xk = sb2("xk", [128, 1024], F32)
        outb = sb2("outb", [128, 1024], F32)
        s.barrier(ph1_end)
        w2toks = []
        wi2 = 0
        HF2 = DFF // 2
        for wname, wtgt in (("w1a", w1b), ("w2a", w2b)):
            for c in range(DC):
                for hf in range(2):
                    if wi2 >= 1:
                        s.wait("sync", w2toks[wi2 - 1])
                    ld = s.dma("sync", wst2[:],
                               dr[wname][128 * c:128 * (c + 1), HF2 * hf:HF2 * (hf + 1)], "DX")
                    if wi2 % 2 == 0:
                        s.wait("vector", ld)
                        tk = vcp(wtgt[:, DFF * c + HF2 * hf:DFF * c + HF2 * (hf + 1)],
                                 wst2[:], inc="SV")
                    else:
                        s.wait("scalar", ld)
                        tk = act(wtgt[:, DFF * c + HF2 * hf:DFF * c + HF2 * (hf + 1)],
                                 wst2[:], AF.Copy)
                    w2toks.append(tk)
                    wi2 += 1
        for c in range(NFC):
            cw = FCW[c]
            if wi2 >= 1:
                s.wait("sync", w2toks[wi2 - 1])
            ld = s.dma("sync", wst2[0:cw, 0:D], dr["w3a"][128 * c:128 * c + cw, :], "DX")
            if wi2 % 2 == 0:
                s.wait("vector", ld)
                tk = vcp(w3b[0:cw, D * c:D * (c + 1)], wst2[0:cw, 0:D], inc="SV")
            else:
                s.wait("scalar", ld)
                tk = act(w3b[0:cw, D * c:D * (c + 1)], wst2[0:cw, 0:D], AF.Copy)
            w2toks.append(tk)
            wi2 += 1
        wrdy = [s.tok("SV"), s.tok("SA")]
        for i in range(NT):
            t0 = i * TT
            if i > 0:
                s.wait("sync", s.tok("ST"), s.tok("SV"))
                s.wait("vector", s.tok("ST"))
            x1toks = []
            for c in range(DC):
                x1toks.append(s.dma("sync", x1r[:, TT * c:TT * (c + 1)],
                                    x1s[128 * c:128 * (c + 1), t0:t0 + TT], "DX"))
            r2l = s.dma("sync", rows[0:1, 0:TT], r2s[0:1, t0:t0 + TT], "DX")
            s.wait("vector", r2l)
            rb = vcp(rowbf[0:1, 0:TT], rows[0:1, 0:TT], inc="SV")
            s.wait("tensor", rb, *wrdy)
            bm = mm(psA[0:128, 0:TT], conesr[:], rowbf[0:1, 0:TT], True, True, inc="ST")
            s.wait("scalar", bm)
            r2e = act(s1t[:], psA[0:128, 0:TT], AF.Copy)
            s.wait("vector", r2e)
            for c in range(DC):
                s.wait("vector", x1toks[c])
                s.do("vector", lambda e, c=c: e.scalar_tensor_tensor(
                    out=h2t[:, TT * c:TT * (c + 1)], in0=x1r[:, TT * c:TT * (c + 1)],
                    scalar=gf[:, c:c + 1], in1=s1t[:], op0=ALU.mult, op1=ALU.mult),
                    inc="SV", r=[x1r, s1t, gf], w=[h2t])
            h2tok = s.tok("SV")
            s.wait("tensor", h2tok)
            ev2 = []
            for oc in range(NFC if _P2STOP != "h2" else 0):
                cw = FCW[oc]
                pb1 = psA if oc % 2 == 0 else psB
                pb2 = psO if oc % 2 == 0 else psK
                if oc >= 2:
                    s.wait("tensor", ev2[oc - 2])
                for c in range(DC):
                    mm(pb1[0:cw, 0:TT], w1b[:, DFF * c + 128 * oc:DFF * c + 128 * oc + cw],
                       h2t[:, TT * c:TT * (c + 1)], c == 0, c == DC - 1,
                       inc="ST" if c == DC - 1 else None)
                for c in range(DC):
                    mm(pb2[0:cw, 0:TT], w2b[:, DFF * c + 128 * oc:DFF * c + 128 * oc + cw],
                       h2t[:, TT * c:TT * (c + 1)], c == 0, c == DC - 1,
                       inc="ST" if c == DC - 1 else None)
                umm = s.tok("ST")
                s.wait("scalar", umm)
                if oc >= 1:
                    s.wait("scalar", ev2[oc - 1])
                sl = act(s1t[0:cw, 0:TT], pb1[0:cw, 0:TT], AF.Silu)
                s.wait("vector", sl)
                tk = s.do("vector", lambda e, oc=oc, cw=cw, pb2=pb2: e.tensor_tensor(
                    out=sgl[0:cw, TT * oc:TT * (oc + 1)],
                    in0=s1t[0:cw, 0:TT],
                    in1=pb2[0:cw, 0:TT], op=ALU.mult), inc="SV", r=[s1t, pb2], w=[sgl])
                ev2.append(tk)
            sgl_tok = s.tok("SV")
            ptk = None
            tptk = None
            outw = None
            for sub in range(4):
                if outw is not None:
                    s.wait("sync", tptk)
                xl = s.dma("sync", xk[:], xtok[t0 + 128 * sub:t0 + 128 * (sub + 1), :], "DX")
                s.wait("tensor", sgl_tok)
                if ptk is not None:
                    s.wait("tensor", ptk)
                for c in range(NFC):
                    cw = FCW[c]
                    mm(psW[0:128, 0:512],
                       sgl[0:cw, TT * c + 128 * sub:TT * c + 128 * (sub + 1)],
                       w3b[0:cw, D * c:D * c + 512], c == 0, c == NFC - 1)
                    mm(psW[0:128, 512:1024],
                       sgl[0:cw, TT * c + 128 * sub:TT * c + 128 * (sub + 1)],
                       w3b[0:cw, D * c + 512:D * (c + 1)], c == 0, c == NFC - 1,
                       inc="ST" if c == NFC - 1 else None)
                w3mm = s.tok("ST")
                if tptk is not None:
                    s.wait("tensor", tptk)
                for c in range(DC):
                    tpose(psT[:, 128 * c:128 * (c + 1)],
                          x1r[:, TT * c + 128 * sub:TT * c + 128 * (sub + 1)],
                          inc="ST" if c == DC - 1 else None)
                tpm = s.tok("ST")
                s.wait("vector", tpm, xl)
                if outw is not None:
                    s.wait("vector", outw)
                tptk = vtt(outb[:], xk[:], psT[:, 0:1024], ALU.add, inc="SV")
                s.wait("vector", w3mm)
                ptk = vtt(outb[:], outb[:], psW[0:128, 0:1024], ALU.add, inc="SV")
                s.wait("gpsimd", ptk)
                outw = s.dma("gpsimd", out_d[t0 + 128 * sub:t0 + 128 * (sub + 1), :],
                             outb[:], "DG")
        ph2_end = s.all_toks()
        ph2.close()
    else:
        ph2_end = s.all_toks()

    if 3 in _PHASES:
        ph3 = contextlib.ExitStack()

        def sb3(name, shape, dt):
            return ph3.enter_context(nc.sbuf_tensor(name, shape, dt))

        f1q = sb3("f1q", [128, DC * H4], FP8)
        f2q = sb3("f2q", [128, H4C * D], FP8)
        wst3 = sb3("wst3", [128, H4 // 2], F32)
        xq8r = sb3("xq8r", [128, TT * DC], FP8)
        h16t = sb3("h16t", [128, 2 * TT], FP16)
        hq8 = sb3("hq8", [128, H4C * TT], FP8)
        mk = sb3("mk", [128, 2 * 1024], BF16)
        pbf = sb3("pbf", [128, 2 * 1024], F32)
        y16 = sb3("y16", [128, 1024], FP16)
        otb = sb3("otb", [128, 2 * 1024], F32)
        s.barrier(ph2_end)
        w3t = []
        wi3 = 0
        for c in range(DC):
            for hf in range(2):
                if wi3 >= 1:
                    s.wait("sync", w3t[wi3 - 1])
                ld = s.dma("sync", wst3[:],
                           dr["fw1t"][128 * c:128 * (c + 1), 2048 * hf:2048 * (hf + 1)], "DX")
                s.wait("scalar", ld)
                w3t.append(act(f1q[:, H4 * c + 2048 * hf:H4 * c + 2048 * (hf + 1)],
                               wst3[:], AF.Copy))
                wi3 += 1
        for c in range(H4C):
            if wi3 >= 1:
                s.wait("sync", w3t[wi3 - 1])
            ld = s.dma("sync", wst3[:, 0:D], dr["fw2t"][128 * c:128 * (c + 1), :], "DX")
            s.wait("scalar", ld)
            w3t.append(act(f2q[:, D * c:D * (c + 1)], wst3[:, 0:D], AF.Copy))
            wi3 += 1
        wrdy3 = s.tok("SA")
        for i in range(NT):
            t0 = i * TT
            if i > 0:
                s.wait("sync", s.tok("ST"))
            xqt = []
            for c in range(DC):
                xqt.append(s.dma("sync", xq8r[:, TT * c:TT * (c + 1)],
                                 xq8s[128 * c:128 * (c + 1), t0:t0 + TT], "DX"))
            s.wait("tensor", wrdy3, *xqt)
            hev = []
            for oc in range(H4C):
                pban = psA if oc % 2 == 0 else psB
                if oc >= 2:
                    s.wait("tensor", hev[oc - 2])
                for c in range(DC):
                    mm(pban[0:128, 0:TT], f1q[:, H4 * c + 128 * oc:H4 * c + 128 * (oc + 1)],
                       xq8r[:, TT * c:TT * (c + 1)], c == 0, c == DC - 1,
                       inc="ST" if c == DC - 1 else None)
                m1t = s.tok("ST")
                s.wait("scalar", m1t)
                act(h16t[:, TT * (oc % 2):TT * (oc % 2 + 1)], pban[0:128, 0:TT],
                    AF.Identity, bias=b1q[:, oc:oc + 1])
                hev.append(act(hq8[:, TT * oc:TT * (oc + 1)],
                               h16t[:, TT * (oc % 2):TT * (oc % 2 + 1)], AF.Relu))
            hq8_tok = s.tok("SA")
            ftk = None
            fhist = []
            ohist = []
            for sub in range(4):
                pp = sub % 2
                if sub >= 2:
                    s.wait("sync", fhist[sub - 2])
                ml = s.dma("sync", mk[:, 1024 * pp:1024 * (pp + 1)],
                           mask_d[t0 + 128 * sub:t0 + 128 * (sub + 1), :], "DX")
                pl = s.dma("sync", pbf[:, 1024 * pp:1024 * (pp + 1)],
                           out_d[t0 + 128 * sub:t0 + 128 * (sub + 1), :], "DX")
                s.wait("tensor", hq8_tok)
                if ftk is not None:
                    s.wait("tensor", ftk)
                for c in range(H4C):
                    mm(psW[0:128, 0:512],
                       hq8[:, TT * c + 128 * sub:TT * c + 128 * (sub + 1)],
                       f2q[:, D * c:D * c + 512], c == 0, c == H4C - 1)
                    mm(psW[0:128, 512:1024],
                       hq8[:, TT * c + 128 * sub:TT * c + 128 * (sub + 1)],
                       f2q[:, D * c + 512:D * (c + 1)], c == 0, c == H4C - 1,
                       inc="ST" if c == H4C - 1 else None)
                m2t = s.tok("ST")
                s.wait("vector", m2t, ml, pl)
                if sub >= 2:
                    s.wait("vector", ohist[sub - 2])
                vtt(y16[:], psW[0:128, 0:1024], b2tile[:], ALU.add)
                vtt(otb[:, 1024 * pp:1024 * (pp + 1)], y16[:],
                    mk[:, 1024 * pp:1024 * (pp + 1)], ALU.mult)
                ftk = vtt(otb[:, 1024 * pp:1024 * (pp + 1)],
                          otb[:, 1024 * pp:1024 * (pp + 1)],
                          pbf[:, 1024 * pp:1024 * (pp + 1)], ALU.add, inc="SV")
                fhist.append(ftk)
                s.wait("gpsimd", ftk)
                ohist.append(s.dma("gpsimd", out_d[t0 + 128 * sub:t0 + 128 * (sub + 1), :],
                                   otb[:, 1024 * pp:1024 * (pp + 1)], "DG"))
        ph3.close()

    s.wait("gpsimd", s.tok("DG"))
    s.wait("sync", s.tok("DX"))

    block = es.enter_context(nc.Block())
    s.realize(block)
    es.close()
    return nc


def _consts():
    cid = np.eye(128, dtype=BF)
    cmask = np.fromfunction(lambda r, c: ((r % 64) <= (c % 64)), (128, 128)).astype(BF)
    cob = np.fromfunction(lambda r, c: ((r // 64) == (c // 64)), (128, 128)).astype(BF)
    cones = np.ones((128, 1), BF)
    conesr = np.ones((1, 128), BF)
    cstart = np.zeros((128, TT), BF)
    cstart[:, ::C] = 1
    cbig = np.full((128, TT), 1e36, BF)
    cbig[:, ::C] = 0
    return cid, cmask, cob, cones, conesr, cstart, cbig


def kernel(L0_attn_x, L0_attn_Wq, L0_attn_Wk, L0_attn_Wv, L0_attn_Wg, L0_attn_Wo,
           L0_attn_w1, L0_attn_w2, L0_attn_w3, L0_attn_g_attn, L0_attn_g_ffn,
           L0_ffn_w1, L0_ffn_b1, L0_ffn_w2, L0_ffn_b2):
    x = np.asarray(L0_attn_x, np.float32)
    B = x.shape[0]
    maskml = _dropout_maskmul((B, T, D))
    cid, cmask, cob, cones, conesr, cstart, cbig = _consts()
    shared = {
        "wq": np.ascontiguousarray(L0_attn_Wq, dtype=np.float32),
        "wk": np.ascontiguousarray(L0_attn_Wk, dtype=np.float32),
        "wv": np.ascontiguousarray(L0_attn_Wv, dtype=np.float32),
        "wg": np.ascontiguousarray(L0_attn_Wg, dtype=np.float32),
        "wo": np.ascontiguousarray(L0_attn_Wo, dtype=np.float32),
        "w1a": np.ascontiguousarray(L0_attn_w1, dtype=np.float32),
        "w2a": np.ascontiguousarray(L0_attn_w2, dtype=np.float32),
        "w3a": np.ascontiguousarray(L0_attn_w3, dtype=np.float32),
        "fw1t": np.ascontiguousarray(np.asarray(L0_ffn_w1, np.float32).T),
        "fw2t": np.ascontiguousarray(np.asarray(L0_ffn_w2, np.float32).T),
        "ga": np.ascontiguousarray(np.asarray(L0_attn_g_attn, np.float32).reshape(DC, 128).T),
        "gf": np.ascontiguousarray(np.asarray(L0_attn_g_ffn, np.float32).reshape(DC, 128).T),
        "fb1": np.ascontiguousarray(np.asarray(L0_ffn_b1, np.float32).reshape(H4C, 128).T),
        "fb2": np.ascontiguousarray(np.asarray(L0_ffn_b2, np.float32).reshape(1, D)),
        "cid": cid, "cmask": cmask, "cob": cob, "cones": cones, "conesr": conesr,
        "cstart": cstart, "cbig": cbig, "conesrf": np.ones((1, 128), np.float32),
        "ceps": np.array([[1e-5, 1e-6]], np.float32),
    }
    nc = build_program()
    in_maps = []
    for b in range(B):
        m = dict(shared)
        m["xfm"] = np.ascontiguousarray(x[b].T)
        m["xtok"] = np.ascontiguousarray(x[b])
        m["maskml"] = np.ascontiguousarray(maskml[b])
        in_maps.append(m)
    res = run_bass_kernel_spmd(nc, in_maps, core_ids=list(range(B)))
    return np.stack([np.asarray(r["out"], np.float32) for r in res.results], axis=0)


# version 05
# speedup vs baseline: 1.0008x; 1.0008x over previous
import contextlib

import numpy as np
import ml_dtypes

import concourse.bass as bass
import concourse.mybir as mybir
from concourse.bass_utils import run_bass_kernel_spmd

F32 = mybir.dt.float32
BF16 = mybir.dt.bfloat16
FP16 = mybir.dt.float16
FP8 = mybir.dt.float8e4
AF = mybir.ActivationFunctionType
ALU = mybir.AluOpType
BF = ml_dtypes.bfloat16
E4 = ml_dtypes.float8_e4m3

T, D, NH, DK = 4096, 1024, 16, 64
TT = 512
NT = T // TT
C = 64
NCH = TT // C
NP = 8
DC = 8
DFF = 2736
NFC = 22
FCW = [128] * 21 + [48]
H4 = 4096
H4C = 32

ENGS = ("sync", "tensor", "vector", "scalar", "gpsimd")

_PHASES = (1, 2, 3)
_P1STOP = None
_SCANPART = 'full'
_P2STOP = None
_NT = NT
_DBG = {}


def _threefry2x32(k1, k2, x0, x1):
    rot = [np.array([13, 15, 26, 6], np.uint32), np.array([17, 29, 16, 24], np.uint32)]
    ks = [np.uint32(k1), np.uint32(k2), np.uint32(k1) ^ np.uint32(k2) ^ np.uint32(0x1BD11BDA)]
    x = [x0.astype(np.uint32).copy(), x1.astype(np.uint32).copy()]
    x[0] += ks[0]
    x[1] += ks[1]

    def r_(v, d):
        d = np.uint32(d)
        return (v << d) | (v >> np.uint32(32 - d))

    def rounds(x, rs):
        for rr in rs:
            x[0] += x[1]
            x[1] = r_(x[1], rr)
            x[1] ^= x[0]
        return x

    x = rounds(x, rot[0]); x[0] += ks[1]; x[1] += ks[2] + np.uint32(1)
    x = rounds(x, rot[1]); x[0] += ks[2]; x[1] += ks[0] + np.uint32(2)
    x = rounds(x, rot[0]); x[0] += ks[0]; x[1] += ks[1] + np.uint32(3)
    x = rounds(x, rot[1]); x[0] += ks[1]; x[1] += ks[2] + np.uint32(4)
    x = rounds(x, rot[0]); x[0] += ks[2]; x[1] += ks[0] + np.uint32(5)
    return x[0], x[1]


def _dropout_maskmul(shape, p=0.8, seed=42):
    size = int(np.prod(shape))
    idx = np.arange(size, dtype=np.uint64)
    c1 = (idx >> np.uint64(32)).astype(np.uint32)
    c2 = (idx & np.uint64(0xFFFFFFFF)).astype(np.uint32)
    b1, b2 = _threefry2x32(np.uint32(0), np.uint32(seed), c1, c2)
    bits = b1 ^ b2
    fb = (bits >> np.uint32(9)) | np.uint32(0x3F800000)
    u = fb.view(np.float32) - np.float32(1.0)
    keep = (u < np.float32(p)).reshape(shape)
    return np.where(keep, np.float32(1.25), np.float32(0.0)).astype(BF)


class Sched:
    def __init__(self, nc, es, semnames):
        self.nc = nc
        self.ops = {e: [] for e in ENGS}
        self.sem = {}
        self.cnt = {}
        self._vw = [set(), set()]
        for n in semnames:
            self.sem[n] = es.enter_context(nc.semaphore(n))
            self.cnt[n] = 0

    def tok(self, sem):
        return (sem, self.cnt[sem])

    def wait(self, eng, *xs):
        for x in xs:
            if isinstance(x, tuple):
                sm, c = x
            else:
                sm, c = x, self.cnt[x]
            if c > 0:
                self.ops[eng].append(("w", sm, c))

    @staticmethod
    def _names(aps):
        out = set()
        for a in aps:
            t = getattr(a, "tensor", a)
            n = getattr(t, "name", None)
            if n is not None:
                out.add(n)
        return out

    def do(self, eng, fn, inc=None, n=1, r=(), w=()):
        if eng == "vector":
            rd = self._names(r)
            if rd & (self._vw[0] | self._vw[1]):
                self.ops[eng].append(("d",))
                self._vw = [set(), set()]
            self._vw = [self._names(w), self._vw[0]]
        if inc is not None:
            self.cnt[inc] += n
        self.ops[eng].append(("o", fn, inc, n))
        return self.tok(inc) if inc is not None else None

    def dma(self, eng, out, in_, inc):
        return self.do(eng, lambda e, o=out, i=in_: e.dma_start(out=o, in_=i), inc, 16)

    def barrier(self, toks):
        for e in ENGS:
            self.wait(e, *toks)

    def all_toks(self):
        return [self.tok(x) for x in ("ST", "SV", "SA", "DG", "DX")]

    def realize(self, block):
        def make(name):
            items = self.ops[name]

            def run(eng):
                floor = {}
                for it in items:
                    if it[0] == "w":
                        _, sm, c = it
                        if floor.get(sm, 0) >= c:
                            continue
                        floor[sm] = c
                        eng.wait_ge(self.sem[sm], c)
                    elif it[0] == "d":
                        eng.drain()
                    else:
                        _, fn, inc, n = it
                        ins = fn(eng)
                        if inc is not None:
                            ins.then_inc(self.sem[inc], n)
            return run

        for e in ENGS:
            getattr(block, e)(make(e))


def build_program():
    nc = bass.Bass("TRN2", target_bir_lowering=False, debug=False,
                   detect_race_conditions=False)
    dr = {}

    def din(name, shape, dt):
        dr[name] = nc.dram_tensor(name, shape, dt, kind="ExternalInput").ap()
        return dr[name]

    xfm = din("xfm", [D, T], F32)
    xtok = din("xtok", [T, D], F32)
    for wn in ("wq", "wk", "wv", "wg", "wo"):
        din(wn, [D, D], F32)
    din("w1a", [D, DFF], F32)
    din("w2a", [D, DFF], F32)
    din("w3a", [DFF, D], F32)
    ga_d = din("ga", [128, DC], F32)
    gf_d = din("gf", [128, DC], F32)
    din("fw1t", [D, H4], F32)
    din("fw2t", [H4, D], F32)
    fb1_d = din("fb1", [128, H4C], F32)
    fb2_d = din("fb2", [1, D], F32)
    mask_d = din("maskml", [T, D], BF16)
    cid_d = din("cid", [128, 128], BF16)
    cmask_d = din("cmask", [128, 128], BF16)
    cob_d = din("cob", [128, 128], BF16)
    cones_d = din("cones", [128, 1], BF16)
    conesr_d = din("conesr", [1, 128], BF16)
    conesrf_d = din("conesrf", [1, 128], F32)
    cstart_d = din("cstart", [128, TT], BF16)
    cbig_d = din("cbig", [128, TT], BF16)
    ceps_d = din("ceps", [1, 2], F32)

    x1s = nc.dram_tensor("x1s", [D, T], BF16, kind="Internal").ap()
    xq8s = nc.dram_tensor("xq8s", [D, T], FP8, kind="Internal").ap()
    r2s = nc.dram_tensor("r2s", [1, T], F32, kind="Internal").ap()
    out_d = nc.dram_tensor("out", [T, D], F32, kind="ExternalOutput").ap()
    dbg_d = {}
    for nm in ("dbg_q", "dbg_x1", "dbg_h", "dbg_o"):
        if _DBG.get(nm):
            dbg_d[nm] = nc.dram_tensor(nm, [D, T], BF16, kind="ExternalOutput").ap()
    if _DBG.get("dbg_rows"):
        dbg_d["dbg_rows"] = nc.dram_tensor("dbg_rows", [128, 2048], BF16,
                                           kind="ExternalOutput").ap()
    if _DBG.get("dbg_rowsf"):
        dbg_d["dbg_rowsf"] = nc.dram_tensor("dbg_rowsf", [2, 2048], F32,
                                            kind="ExternalOutput").ap()

    es = contextlib.ExitStack()

    def sb(name, shape, dt):
        return es.enter_context(nc.sbuf_tensor(name, shape, dt))

    cid = sb("cid_s", [128, 128], BF16)
    cmask = sb("cmask_s", [128, 128], BF16)
    cob = sb("cob_s", [128, 128], BF16)
    cones = sb("cones_s", [128, 1], BF16)
    conesr = sb("conesr_s", [1, 128], BF16)
    conesrf = sb("conesrf_s", [1, 128], F32)
    cstart = sb("cstart_s", [128, TT], BF16)
    cbig = sb("cbig_s", [128, TT], BF16)
    ceps = sb("ceps_s", [1, 2], F32)
    ga = sb("ga_s", [128, DC], F32)
    gf = sb("gf_s", [128, DC], F32)
    b1q = sb("b1q", [128, H4C], F32)
    b1q8 = sb("b1q8", [128, H4C], FP8)
    b2row8 = sb("b2row8", [1, D], FP8)
    b2rowb = sb("b2rowb", [1, D], BF16)
    b2tile = sb("b2tile", [128, D], BF16)
    Sst = sb("Sst", [128, 1024], BF16)
    lamC = sb("lamC", [128, 64], F32)
    rows = sb("rows", [1, 2048], F32)
    rowbf = sb("rowbf", [1, 1024], BF16)
    

    psA = es.enter_context(nc.psum_tensor("psA", [128, 512], F32))
    psB = es.enter_context(nc.psum_tensor("psB", [128, 512], F32))
    psO = es.enter_context(nc.psum_tensor("psO", [128, 512], F32))
    psK = es.enter_context(nc.psum_tensor("psK", [128, 512], F32))
    psW = es.enter_context(nc.psum_tensor("psW", [128, 1024], F32))
    psT = es.enter_context(nc.psum_tensor("psT", [128, 1024], BF16))
    psS = es.enter_context(nc.psum_tensor("psS", [128, 512], F32))

    s = Sched(nc, es, ["DX", "DG", "ST", "SV", "SA", "SG"])

    def mm(psap, lhs, rhs, start, stop, tp=None, inc=None):
        skip = getattr(psap, "offset", 0) >= 4096
        return s.do("tensor", lambda e, o=psap, l=lhs, rr=rhs, st_=start, sp=stop, t=tp,
                    sk=skip:
                    e.matmul(o, l, rr, start=st_, stop=sp, tile_position=t,
                             skip_group_check=sk), inc=inc)

    def tpose(psap, inap, inc=None):
        def f(e, o=psap, i=inap):
            with nc.allow_low_precision(reason="bf16 transpose to bf16 psum"):
                return e.transpose(o, i, cid[:])
        return s.do("tensor", f, inc=inc)

    def act(out, in_, func, bias=0.0, scale=1.0, inc="SA"):
        if isinstance(bias, float) and bias != 0.0:
            bias = ceps[0:1, 0:1] if bias == 1e-5 else ceps[0:1, 1:2]
        return s.do("scalar", lambda e, o=out, i=in_, f=func, b=bias, sc=scale:
                    e.activation(out=o, in_=i, func=f, bias=b, scale=sc), inc=inc)

    def vtt(out, in0, in1, op, inc=None, rn=(), wn=()):
        return s.do("vector", lambda e, o=out, a=in0, b=in1, p_=op:
                    e.tensor_tensor(out=o, in0=a, in1=b, op=p_), inc=inc,
                    r=list(rn) + [in0, in1], w=list(wn) + [out])

    def vrecip(out, in_, inc=None, rn=(), wn=()):
        def f(e, o=out, i=in_):
            with nc.allow_low_precision(reason="bf16 reciprocal by design"):
                return e.reciprocal(out=o, in_=i)
        return s.do("vector", f, inc=inc, r=list(rn) + [in_], w=list(wn) + [out])

    def vcp(out, in_, inc=None):
        return s.do("vector", lambda e, o=out, i=in_: e.tensor_copy(out=o, in_=i),
                    inc=inc, r=[in_], w=[out])

    for dsrc, tgt in ((cid_d, cid), (cmask_d, cmask), (cob_d, cob), (cones_d, cones),
                      (conesr_d, conesr), (conesrf_d, conesrf), (cstart_d, cstart), (cbig_d, cbig),
                      (ceps_d, ceps), (ga_d, ga), (gf_d, gf)):
        s.dma("sync", tgt[:], dsrc, "DX")
    s.dma("sync", b1q[:], fb1_d, "DX")
    s.dma("sync", rows[0:1, 1024:1024 + D], fb2_d, "DX")
    c0 = s.tok("DX")
    s.wait("scalar", c0)
    act(b1q8[:], b1q[:], AF.Copy)
    act(b1q[:], b1q8[:], AF.Copy)
    act(b2row8[:], rows[0:1, 1024:1024 + D], AF.Copy)
    tb2 = act(b2rowb[:], b2row8[:], AF.Copy)
    s.wait("tensor", tb2)
    mm(psW[0:128, 0:512], conesr[:], b2rowb[0:1, 0:512], True, True)
    t1 = mm(psW[0:128, 512:1024], conesr[:], b2rowb[0:1, 512:1024], True, True, inc="ST")
    s.wait("scalar", t1)
    b2done = act(b2tile[:], psW[0:128, 0:1024], AF.Copy)

    ph1 = contextlib.ExitStack()

    def sb1(name, shape, dt):
        return ph1.enter_context(nc.sbuf_tensor(name, shape, dt))

    wqb = sb1("wqb", [128, DC * D], BF16)
    wkb = sb1("wkb", [128, DC * D], BF16)
    wvb = sb1("wvb", [128, DC * D], BF16)
    wgb = sb1("wgb", [128, DC * D], BF16)
    wob = sb1("wob", [128, DC * D], BF16)
    wst = sb1("wst", [128, D], F32)
    wtoks = []
    wi = 0
    for wname, wtgt in (("wq", wqb), ("wk", wkb), ("wg", wgb), ("wv", wvb), ("wo", wob)):
        for c in range(DC):
            if wi >= 1:
                s.wait("sync", wtoks[wi - 1])
            ld = s.dma("sync", wst[:], dr[wname][128 * c:128 * (c + 1), :], "DX")
            if wi % 2 == 0:
                s.wait("vector", ld)
                tk = vcp(wtgt[:, D * c:D * (c + 1)], wst[:], inc="SV")
            else:
                s.wait("scalar", ld)
                tk = act(wtgt[:, D * c:D * (c + 1)], wst[:], AF.Copy)
            wtoks.append(tk)
            wi += 1
    w_ready = [s.tok("SV"), s.tok("SA")]

    xfp = sb1("xfp", [128, 2 * TT], F32)
    xbf = sb1("xbf", [128, TT * DC], BF16)
    xsq = sb1("xsq", [128, 2 * TT], BF16)
    tmp1 = sb1("tmp1", [128, 2 * TT], F32)
    xq8t = sb1("xq8t", [128, 2 * TT], FP8)
    xnb = sb1("xnb", [128, TT * DC], BF16)
    ht = sb1("ht", [128, TT * DC], BF16)
    qb = sb1("qb", [128, 2 * TT], BF16)
    kb = sb1("kb", [128, 2 * TT], BF16)
    ab = sb1("ab", [128, 2 * TT], BF16)
    iat = sb1("iat", [128, TT], BF16)
    d1a = sb1("d1a", [128, TT], BF16)
    d1b = sb1("d1b", [128, TT], BF16)
    lam = sb1("lam", [128, TT], BF16)
    ilam = sb1("ilam", [128, TT], BF16)
    sqt = sb1("sqt", [128, TT], BF16)
    dqM = sb1("dqM", [128, TT], BF16)
    dkM = sb1("dkM", [128, TT], BF16)
    khf = sb1("khf", [128, TT], BF16)
    Qt = sb1("Qt", [128, TT * NP], BF16)
    Kt = sb1("Kt", [128, TT * NP], BF16)
    Khtok = sb1("Khtok", [128, TT * DC], BF16)
    Vtok = sb1("Vtok", [128, TT * DC], BF16)
    ptl = sb1("ptl", [128, 256], BF16)
    Ofm = sb1("Ofm", [128, TT * NP], BF16)
    x1t = sb1("x1t", [128, TT * DC], BF16)

    s.do("vector", lambda e: e.memset(Sst[:], 0.0), inc="SV", w=[Sst])
    s.barrier(w_ready + [s.tok("SA")])

    mask_toks = {}
    upd_toks = {}

    def emit_cell_head(g, ch, p, ci, prev):
        par = ch % 2
        b = 64 * par
        pr = slice(b, b + 64)
        tc = slice(TT * p + C * ch, TT * p + C * (ch + 1))
        pc = 64 * (ci % 2)
        if prev is not None:
            s.wait("tensor", mask_toks[(prev[0], prev[2])])
        mm(psW[pr, pc:pc + 64], Kt[0:64, tc], Qt[0:64, tc], True, True, (0, b))
        return mm(psW[pr, 512 + pc:512 + pc + 64], Kt[64:128, tc], Qt[64:128, tc],
                  True, True, (64, b), inc="ST")

    def emit_cell_mask(g, ch, p, ci, pmm):
        par = ch % 2
        b = 64 * par
        pr = slice(b, b + 64)
        pc = 64 * (ci % 2)
        ps_ = 128 * (ci % 2)
        s.wait("vector", pmm)
        mask_toks[(g, p)] = s.do(
            "vector", lambda e, pr=pr, pc=pc, ps_=ps_: e.tensor_tensor(
                out=ptl[pr, ps_:ps_ + 128].rearrange("q (a x) -> q a x", a=2),
                in0=psW[pr, 0:1024].rearrange("q (a x) -> q a x", a=2)[:, :, pc:pc + 64],
                in1=cmask[pr, 0:128].rearrange("q (a x) -> q a x", a=2),
                op=ALU.mult),
            inc="SV", r=[psW, cmask], w=[ptl])

    def emit_cell_tail(i, g, ch, p, ci):
        if _SCANPART == "head":
            return
        par = ch % 2
        b = 64 * par
        pr = slice(b, b + 64)
        tc = slice(TT * p + C * ch, TT * p + C * (ch + 1))
        sps = 512 * (g % 2)
        spd = 512 * ((g + 1) % 2)
        vbase = 1024 * (ch // 2) + 128 * p
        pc = 64 * (ci % 2)
        ps_ = 128 * (ci % 2)
        s.wait("tensor", mask_toks[(g, p)])
        mm(psO[0:64, pc:pc + 64], Vtok[pr, vbase:vbase + 64], ptl[pr, ps_:ps_ + 64],
           True, True, (b, 0))
        mm(psO[64:128, pc:pc + 64], Vtok[pr, vbase + 64:vbase + 128],
           ptl[pr, ps_ + 64:ps_ + 128], True, True, (b, 64))
        mm(psK[0:64, pc:pc + 64], Khtok[pr, vbase:vbase + 64], Vtok[pr, vbase:vbase + 64],
           True, True, (b, 0))
        kvm = mm(psK[64:128, pc:pc + 64], Khtok[pr, vbase + 64:vbase + 128],
                 Vtok[pr, vbase + 64:vbase + 128], True, True, (b, 64), inc="ST")
        if (g - 1, p) in upd_toks:
            s.wait("tensor", upd_toks[(g - 1, p)])
        mm(psS[0:64, pc:pc + 64], Sst[0:64, sps + 64 * p:sps + 64 * (p + 1)],
           Qt[0:64, tc], True, True, (0, 0))
        qsm = mm(psS[64:128, pc:pc + 64], Sst[64:128, sps + 64 * p:sps + 64 * (p + 1)],
                 Qt[64:128, tc], True, True, (64, 64), inc="ST")
        s.wait("scalar", qsm)
        ocp = s.do("scalar", lambda e, tc=tc, pc=pc: e.activation(
            out=Ofm[:, tc], in_=psO[:, pc:pc + 64], func=AF.Copy, bias=0.0, scale=1.0),
            inc="SA")
        s.wait("vector", ocp)
        s.do("vector", lambda e, tc=tc, pc=pc: e.tensor_tensor(
            out=Ofm[:, tc], in0=Ofm[:, tc], in1=psS[:, pc:pc + 64], op=ALU.add),
            inc="SV", r=[Ofm, psS], w=[Ofm])
        s.wait("vector", kvm)
        upd_toks[(g, p)] = s.do(
            "vector", lambda e, p=p, ch=ch, sps=sps, spd=spd, pc=pc: e.scalar_tensor_tensor(
                out=Sst[:, spd + 64 * p:spd + 64 * (p + 1)],
                in0=Sst[:, sps + 64 * p:sps + 64 * (p + 1)],
                scalar=lamC[:, 8 * p + ch:8 * p + ch + 1],
                in1=psK[:, pc:pc + 64], op0=ALU.mult, op1=ALU.add),
            inc="SV", r=[Sst, lamC, psK], w=[Sst])

    def emit_p1_tile(i, prev_toks):
        t0 = i * TT
        if prev_toks is not None:
            s.wait("vector", prev_toks[0], prev_toks[2])
            s.wait("scalar", prev_toks[0])
            s.wait("sync", prev_toks[1])
            s.wait("tensor", prev_toks[1])
        xtoks = []
        cast_toks = []
        sum_toks = []
        for c in range(DC):
            pp = c % 2
            if c >= 2:
                s.wait("sync", cast_toks[c - 2])
            xtoks.append(s.dma("sync", xfp[:, TT * pp:TT * (pp + 1)],
                               xfm[128 * c:128 * (c + 1), t0:t0 + TT], "DX"))
            s.wait("scalar", xtoks[c])
            cast_toks.append(act(xbf[:, TT * c:TT * (c + 1)],
                                 xfp[:, TT * pp:TT * (pp + 1)], AF.Copy))
            if c >= 2:
                s.wait("scalar", sum_toks[c - 2])
            sqk = act(xsq[:, TT * (c % 2):TT * (c % 2 + 1)],
                      xbf[:, TT * c:TT * (c + 1)], AF.Square)
            s.wait("tensor", sqk)
            sum_toks.append(mm(psS[0:1, 0:TT], cones[:],
                               xsq[:, TT * (c % 2):TT * (c % 2 + 1)],
                               c == 0, False if c < DC - 1 else True, inc="ST"))
        sx2 = s.tok("ST")
        s.wait("vector", sx2)
        s.do("vector", lambda e: e.tensor_scalar_mul(
            out=rows[0:1, TT:2 * TT], in0=psS[0:1, 0:TT], scalar1=1.0 / D),
            inc="SV", r=[psS], w=[rows])
        ev1 = s.tok("SV")
        for c in range(DC):
            s.wait("tensor", ev1)
            mm(psS[0:1, 0:TT], cones[:], xbf[:, TT * c:TT * (c + 1)],
               c == 0, c == DC - 1, inc="ST" if c == DC - 1 else None)
        sx1 = s.tok("ST")
        s.wait("vector", sx1)
        mevtk = s.do("vector", lambda e: e.tensor_scalar_mul(
            out=rows[0:1, 0:TT], in0=psS[0:1, 0:TT], scalar1=1.0 / D),
            inc="SV", r=[psS], w=[rows])
        if "dbg_rowsf" in dbg_d and i == 0:
            s.wait("gpsimd", mevtk)
            s.dma("gpsimd", dbg_d["dbg_rowsf"][0:1, :], rows[0:1, :], "DG")
            s.wait("vector", s.tok("DG"))
        vtt(rows[0:1, 2 * TT:3 * TT], rows[0:1, 0:TT], rows[0:1, 0:TT], ALU.mult)
        vrow = vtt(rows[0:1, 2 * TT:3 * TT], rows[0:1, TT:2 * TT],
                   rows[0:1, 2 * TT:3 * TT], ALU.subtract, inc="SV")
        s.wait("scalar", vrow)
        sq1 = act(rows[0:1, 3 * TT:4 * TT], rows[0:1, 2 * TT:3 * TT], AF.Sqrt, bias=1e-5)
        s.wait("vector", sq1)
        s.do("vector", lambda e: e.reciprocal(out=rows[0:1, TT:2 * TT],
                                              in_=rows[0:1, 3 * TT:4 * TT]),
             inc="SV", r=[rows], w=[rows])
        vtt(rows[0:1, 3 * TT:4 * TT], rows[0:1, 0:TT], rows[0:1, TT:2 * TT], ALU.mult)
        vtt(rows[0:1, 0:TT], rows[0:1, 2 * TT:3 * TT], rows[0:1, TT:2 * TT], ALU.mult)
        vs2 = vtt(rows[0:1, 0:TT], rows[0:1, 0:TT], rows[0:1, TT:2 * TT], ALU.mult, inc="SV")
        s.wait("scalar", vs2)
        sq2 = act(rows[0:1, 2 * TT:3 * TT], rows[0:1, 0:TT], AF.Sqrt, bias=1e-6)
        s.wait("vector", sq2)
        s.do("vector", lambda e: e.reciprocal(out=rows[0:1, 0:TT],
                                              in_=rows[0:1, 2 * TT:3 * TT]),
             inc="SV", r=[rows], w=[rows])
        if "dbg_rowsf" in dbg_d and i == 0:
            s.wait("gpsimd", s.tok("SV"))
            s.dma("gpsimd", dbg_d["dbg_rowsf"][1:2, :], rows[0:1, :], "DG")
            s.wait("vector", s.tok("DG"))
        for src0, tgt in ((TT, psA), (3 * TT, psB), (0, psO)):
            vcp(rowbf[0:1, 0:TT], rows[0:1, src0:src0 + TT])
            vtt(rows[0:1, 2 * TT:3 * TT], rows[0:1, src0:src0 + TT],
                rowbf[0:1, 0:TT], ALU.subtract)
            hl = vcp(rowbf[0:1, TT:2 * TT], rows[0:1, 2 * TT:3 * TT], inc="SV")
            s.wait("tensor", hl)
            mm(tgt[0:128, 0:TT], conesr[:], rowbf[0:1, 0:TT], True, False)
            bm = mm(tgt[0:128, 0:TT], conesr[:], rowbf[0:1, TT:2 * TT], False, True,
                    inc="ST")
            s.wait("vector", bm)
        if "dbg_rows" in dbg_d and i == 0:
            s.wait("scalar", bm)
            act(xsq[:, 0:TT], psA[0:128, 0:TT], AF.Copy)
            act(xsq[:, TT:2 * TT], psB[0:128, 0:TT], AF.Copy)
            dr8 = act(xbf[:, 0:TT], psO[0:128, 0:TT], AF.Copy)
            s.wait("gpsimd", dr8)
            s.dma("gpsimd", dbg_d["dbg_rows"][:, 0:TT], xsq[:, 0:TT], "DG")
            s.dma("gpsimd", dbg_d["dbg_rows"][:, TT:2 * TT], xsq[:, TT:2 * TT], "DG")
            s.dma("gpsimd", dbg_d["dbg_rows"][:, 2 * TT:3 * TT], xbf[:, 0:TT], "DG")
            s.dma("gpsimd", dbg_d["dbg_rows"][0:1, 3 * TT:4 * TT], rows[0:1, TT:2 * TT], "DG")
            s.wait("vector", s.tok("DG"))
            s.wait("scalar", s.tok("DG"))
        s.wait("vector", bm)
        xn_sv = []
        q8dma = []
        xdma2 = []
        xnb_acts = []
        for c in range(DC):
            pp = c % 2
            if c >= 2:
                s.wait("sync", xn_sv[c - 2])
            else:
                s.wait("sync", cast_toks[DC - 2 + c])
            xdma2.append(s.dma("sync", xfp[:, TT * pp:TT * (pp + 1)],
                               xfm[128 * c:128 * (c + 1), t0:t0 + TT], "DX"))
            s.wait("vector", xdma2[c])
            if c >= 2:
                s.wait("vector", xnb_acts[c - 2])
            vtt(tmp1[:, TT * pp:TT * (pp + 1)], xfp[:, TT * pp:TT * (pp + 1)],
                psA[0:128, 0:TT], ALU.mult)
            x2t = vtt(tmp1[:, TT * pp:TT * (pp + 1)], tmp1[:, TT * pp:TT * (pp + 1)],
                      psB[0:128, 0:TT], ALU.subtract, inc="SV")
            s.wait("scalar", x2t)
            if 3 in _PHASES:
                if c >= 2 and q8dma[c - 2] is not None:
                    s.wait("scalar", q8dma[c - 2])
                q8 = act(xq8t[:, TT * pp:TT * (pp + 1)], tmp1[:, TT * pp:TT * (pp + 1)],
                         AF.Copy)
                s.wait("gpsimd", q8)
                q8dma.append(s.dma("gpsimd", xq8s[128 * c:128 * (c + 1), t0:t0 + TT],
                                   xq8t[:, TT * pp:TT * (pp + 1)], "DG"))
            xnb_acts.append(act(xnb[:, TT * c:TT * (c + 1)], tmp1[:, TT * pp:TT * (pp + 1)],
                                AF.Copy))
            vtt(xsq[:, TT * pp:TT * (pp + 1)], tmp1[:, TT * pp:TT * (pp + 1)],
                psO[0:128, 0:TT], ALU.mult)
            htt = s.do("vector", lambda e, c=c, pp=pp: e.tensor_scalar(
                out=ht[:, TT * c:TT * (c + 1)], in0=xsq[:, TT * pp:TT * (pp + 1)],
                scalar1=ga[:, c:c + 1], scalar2=None, op0=ALU.mult),
                inc="SV", r=[xsq, ga], w=[ht])
            xn_sv.append(htt)
        htok = s.tok("SV")
        s.wait("tensor", s.tok("SA"))
        if _P1STOP == "ln":
            return [s.tok("ST"), s.tok("SV"), s.tok("DG")]
        evac_toks = {}
        tp_toks = {}
        prep_psO = None
        s.wait("tensor", htok)
        prep_sv = {}
        for p in range(NP):
            pp = p % 2
            pban, poth = (psA, psB) if pp == 0 else (psB, psA)
            if p >= 1:
                s.wait("tensor", evac_toks[p - 1])
            if p >= 2:
                s.wait("tensor", prep_sv[p - 2])
            for c in range(DC):
                mm(pban[0:128, 0:TT], wqb[:, D * c + 128 * p:D * c + 128 * (p + 1)],
                   ht[:, TT * c:TT * (c + 1)], c == 0, c == DC - 1,
                   inc="ST" if c == DC - 1 else None)
            qmm = s.tok("ST")
            for c in range(DC):
                mm(poth[0:128, 0:TT], wkb[:, D * c + 128 * p:D * c + 128 * (p + 1)],
                   ht[:, TT * c:TT * (c + 1)], c == 0, c == DC - 1,
                   inc="ST" if c == DC - 1 else None)
            kmm = s.tok("ST")
            s.wait("scalar", qmm)
            qev = act(qb[:, TT * pp:TT * (pp + 1)], pban[0:128, 0:TT], AF.Copy)
            s.wait("scalar", kmm)
            kev = act(kb[:, TT * pp:TT * (pp + 1)], poth[0:128, 0:TT], AF.Copy)
            s.wait("tensor", qev)
            for c in range(DC):
                mm(pban[0:128, 0:TT], wgb[:, D * c + 128 * p:D * c + 128 * (p + 1)],
                   ht[:, TT * c:TT * (c + 1)], c == 0, c == DC - 1,
                   inc="ST" if c == DC - 1 else None)
            gmm = s.tok("ST")
            s.wait("scalar", gmm)
            gev = act(ab[:, TT * pp:TT * (pp + 1)], pban[0:128, 0:TT], AF.Sigmoid)
            evac_toks[p] = gev
            qsl = qb[:, TT * pp:TT * (pp + 1)]
            ksl = kb[:, TT * pp:TT * (pp + 1)]
            asl = ab[:, TT * pp:TT * (pp + 1)]
            s.wait("scalar", gev)
            sq1p = act(sqt[:], qsl, AF.Square)
            if prep_psO is not None:
                s.wait("tensor", prep_psO)
            s.wait("tensor", sq1p)
            m1 = mm(psO[0:128, 0:TT], cob[:], sqt[:], True, True, inc="ST")
            s.wait("scalar", m1)
            dq1 = act(tmp1[:, 0:TT], psO[0:128, 0:TT], AF.Sqrt)
            s.wait("vector", dq1)
            s.do("vector", lambda e: e.tensor_scalar_max(out=tmp1[:, 0:TT], in0=tmp1[:, 0:TT],
                                                         scalar1=1e-12),
                 r=[tmp1], w=[tmp1])
            vrecip(dqM[:], tmp1[:, 0:TT], inc="SV")
            dq2 = s.tok("SV")
            sq2p = act(sqt[:], ksl, AF.Square)
            s.wait("tensor", sq2p, dq2)
            m2 = mm(psO[0:128, 0:TT], cob[:], sqt[:], True, True, inc="ST")
            s.wait("scalar", m2)
            dk1 = act(tmp1[:, 0:TT], psO[0:128, 0:TT], AF.Sqrt)
            s.wait("vector", dk1)
            s.do("vector", lambda e: e.tensor_scalar_max(out=tmp1[:, 0:TT], in0=tmp1[:, 0:TT],
                                                         scalar1=1e-12),
                 r=[tmp1], w=[tmp1])
            vrecip(dkM[:], tmp1[:, 0:TT], inc="SV")
            prep_psO = s.tok("SA")
            vrecip(iat[:], asl, rn=[ab])
            vtt(d1a[:], asl, cstart[:], ALU.mult, rn=[ab])
            vtt(d1b[:], iat[:], cstart[:], ALU.mult)
            vtt(d1b[:], d1b[:], cbig[:], ALU.add)
            s.do("vector", lambda e, a=asl: e.tensor_tensor_scan(
                out=lam[:], data0=a, data1=d1a[:], initial=1.0,
                op0=ALU.mult, op1=ALU.max), r=[ab, d1a], w=[lam])
            s.do("vector", lambda e: e.tensor_tensor_scan(
                out=ilam[:], data0=iat[:], data1=d1b[:], initial=1.0,
                op0=ALU.mult, op1=ALU.min), r=[iat, d1b], w=[ilam])
            s.do("vector", lambda e, p=p: e.tensor_copy(
                out=lamC[:, 8 * p:8 * (p + 1)],
                in_=lam.rearrange("q (c x) -> q c x", x=C)[:, :, C - 1]),
                r=[lam], w=[lamC])
            vtt(sqt[:], qsl, lam[:], ALU.mult, rn=[qb])
            vtt(Qt[:, TT * p:TT * (p + 1)], sqt[:], dqM[:], ALU.mult)
            vtt(sqt[:], ksl, ilam[:], ALU.mult, rn=[kb])
            vtt(Kt[:, TT * p:TT * (p + 1)], sqt[:], dkM[:], ALU.mult)
            for ch in range(NCH):
                khdone = s.do("vector", lambda e, p=p, ch=ch: e.tensor_scalar(
                    out=khf[:, C * ch:C * (ch + 1)],
                    in0=Kt[:, TT * p + C * ch:TT * p + C * (ch + 1)],
                    scalar1=lamC[:, 8 * p + ch:8 * p + ch + 1], scalar2=None,
                    op0=ALU.mult), inc="SV" if ch == NCH - 1 else None,
                    r=[Kt, lamC], w=[khf])
            prep_sv[p] = khdone
            s.wait("tensor", khdone)
            if p > 0:
                s.wait("tensor", tp_toks[p - 1])
            tb = 512 * (p % 2)
            for sub in range(4):
                tpose(psT[:, tb + 128 * sub:tb + 128 * (sub + 1)],
                      khf[:, 128 * sub:128 * (sub + 1)],
                      inc="ST" if sub == 3 else None)
            tpmm = s.tok("ST")
            s.wait("vector", tpmm)
            tp_toks[p] = s.do("vector", lambda e, p=p, tb=tb: e.tensor_copy(
                out=Khtok.rearrange("q (sx qq x) -> q sx qq x", sx=4, qq=8)[:, :, p, :],
                in_=psT[:, tb:tb + 512].rearrange("q (sx x) -> q sx x", sx=4)),
                inc="SV", r=[psT], w=[Khtok])
        if _P1STOP == "proj":
            return [s.tok("ST"), s.tok("SV"), s.tok("DG")]
        vev = []
        vg = 0
        s.wait("tensor", evac_toks[NP - 1])
        for sub in range(4):
            for hf in range(2):
                pban = psA if vg % 2 == 0 else psB
                if vg >= 2:
                    s.wait("tensor", vev[vg - 2])
                for c in range(DC):
                    mm(pban[0:128, 0:TT],
                       ht[:, TT * c + 128 * sub:TT * c + 128 * (sub + 1)],
                       wvb[:, D * c + 512 * hf:D * c + 512 * (hf + 1)],
                       c == 0, c == DC - 1, inc="ST" if c == DC - 1 else None)
                vmm = s.tok("ST")
                s.wait("vector", vmm)
                tk = s.do("vector", lambda e, sub=sub, hf=hf, pban=pban: e.tensor_copy(
                    out=Vtok[:, 1024 * sub + 512 * hf:1024 * sub + 512 * (hf + 1)],
                    in_=pban[0:128, 0:TT]), inc="SV", r=[pban], w=[Vtok])
                vev.append(tk)
                vg += 1
        s.wait("tensor", s.tok("SV"), s.tok("SA"))
        cells = [(ch, p) for ch in range(NCH) for p in range(NP)]
        prev = None
        for ci, (ch, p) in enumerate(cells):
            g = NCH * i + ch
            pmm = emit_cell_head(g, ch, p, ci, prev)
            if prev is not None:
                pg, pch, ppp, pci = prev
                emit_cell_tail(i, pg, pch, ppp, pci)
            emit_cell_mask(g, ch, p, ci, pmm)
            prev = (g, ch, p, ci)
        if prev is not None:
            emit_cell_tail(i, prev[0], prev[1], prev[2], prev[3])
        if _P1STOP == "scan":
            return [s.tok("ST"), s.tok("SV"), s.tok("DG")]
        sv_now = s.tok("SV")
        s.wait("tensor", sv_now)
        wo_ev = []
        r2sum = []
        for oc in range(DC):
            pban = psA if oc % 2 == 0 else psB
            if oc >= 2:
                s.wait("tensor", wo_ev[oc - 2])
            for p in range(NP):
                mm(pban[0:128, 0:TT], wob[:, D * p + 128 * oc:D * p + 128 * (oc + 1)],
                   Ofm[:, TT * p:TT * (p + 1)], p == 0, p == NP - 1,
                   inc="ST" if p == NP - 1 else None)
            wmm = s.tok("ST")
            s.wait("vector", wmm)
            x1e = vtt(x1t[:, TT * oc:TT * (oc + 1)], pban[0:128, 0:TT],
                      xnb[:, TT * oc:TT * (oc + 1)], ALU.add, inc="SV")
            wo_ev.append(x1e)
            s.wait("scalar", x1e)
            if oc >= 2:
                s.wait("scalar", r2sum[oc - 2])
            xsqe = act(xsq[:, TT * (oc % 2):TT * (oc % 2 + 1)],
                       x1t[:, TT * oc:TT * (oc + 1)], AF.Square)
            s.wait("tensor", xsqe)
            r2sum.append(mm(psS[0:1, 0:TT], cones[:],
                            xsq[:, TT * (oc % 2):TT * (oc % 2 + 1)],
                            oc == 0, oc == DC - 1, inc="ST"))
            if 2 in _PHASES:
                s.wait("gpsimd", x1e)
                s.dma("gpsimd", x1s[128 * oc:128 * (oc + 1), t0:t0 + TT],
                      x1t[:, TT * oc:TT * (oc + 1)], "DG")
        r2mm = s.tok("ST")
        s.wait("scalar", r2mm)
        r2a = act(rows[0:1, 0:TT], psS[0:1, 0:TT], AF.Sqrt, bias=1e-6, scale=1.0 / D)
        s.wait("vector", r2a)
        r2v = s.do("vector", lambda e: e.reciprocal(
            out=rows[0:1, TT:2 * TT], in_=rows[0:1, 0:TT]), inc="SV",
            r=[rows], w=[rows])
        s.wait("gpsimd", r2v)
        s.dma("gpsimd", r2s[0:1, t0:t0 + TT], rows[0:1, TT:2 * TT], "DG")
        for nm, buf in (("dbg_q", Qt), ("dbg_h", ht), ("dbg_o", Ofm), ("dbg_x1", x1t)):
            if nm in dbg_d:
                dd = s.tok("SV")
                s.wait("gpsimd", dd, s.tok("SA"))
                nchunk = buf.shape[1] // TT
                for c in range(min(DC, nchunk)):
                    s.dma("gpsimd", dbg_d[nm][128 * c:128 * (c + 1), t0:t0 + TT],
                          buf[:, TT * c:TT * (c + 1)], "DG")
        return [s.tok("ST"), s.tok("SV"), s.tok("DG")]

    if 1 in _PHASES:
        prev_toks = None
        for i in range(_NT):
            prev_toks = emit_p1_tile(i, prev_toks)

    ph1_end = s.all_toks()
    ph1.close()

    if 2 in _PHASES:
        ph2 = contextlib.ExitStack()

        def sb2(name, shape, dt):
            return ph2.enter_context(nc.sbuf_tensor(name, shape, dt))

        w1b = sb2("w1b", [128, DC * DFF], BF16)
        w2b = sb2("w2b", [128, DC * DFF], BF16)
        w3b = sb2("w3b", [128, NFC * D], BF16)
        wst2 = sb2("wst2", [128, DFF // 2], F32)
        x1r = sb2("x1r", [128, TT * DC], BF16)
        h2t = sb2("h2t", [128, TT * DC], BF16)
        s1t = sb2("s1t", [128, TT], BF16)
        sgl = sb2("sgl", [128, NFC * TT], BF16)
        xk = sb2("xk", [128, 1024], F32)
        outb = sb2("outb", [128, 1024], F32)
        s.barrier(ph1_end)
        w2toks = []
        wi2 = 0
        HF2 = DFF // 2
        for wname, wtgt in (("w1a", w1b), ("w2a", w2b)):
            for c in range(DC):
                for hf in range(2):
                    if wi2 >= 1:
                        s.wait("sync", w2toks[wi2 - 1])
                    ld = s.dma("sync", wst2[:],
                               dr[wname][128 * c:128 * (c + 1), HF2 * hf:HF2 * (hf + 1)], "DX")
                    if wi2 % 2 == 0:
                        s.wait("vector", ld)
                        tk = vcp(wtgt[:, DFF * c + HF2 * hf:DFF * c + HF2 * (hf + 1)],
                                 wst2[:], inc="SV")
                    else:
                        s.wait("scalar", ld)
                        tk = act(wtgt[:, DFF * c + HF2 * hf:DFF * c + HF2 * (hf + 1)],
                                 wst2[:], AF.Copy)
                    w2toks.append(tk)
                    wi2 += 1
        for c in range(NFC):
            cw = FCW[c]
            if wi2 >= 1:
                s.wait("sync", w2toks[wi2 - 1])
            ld = s.dma("sync", wst2[0:cw, 0:D], dr["w3a"][128 * c:128 * c + cw, :], "DX")
            if wi2 % 2 == 0:
                s.wait("vector", ld)
                tk = vcp(w3b[0:cw, D * c:D * (c + 1)], wst2[0:cw, 0:D], inc="SV")
            else:
                s.wait("scalar", ld)
                tk = act(w3b[0:cw, D * c:D * (c + 1)], wst2[0:cw, 0:D], AF.Copy)
            w2toks.append(tk)
            wi2 += 1
        wrdy = [s.tok("SV"), s.tok("SA")]
        for i in range(NT):
            t0 = i * TT
            if i > 0:
                s.wait("sync", s.tok("ST"), s.tok("SV"))
                s.wait("vector", s.tok("ST"))
            x1toks = []
            for c in range(DC):
                x1toks.append(s.dma("sync", x1r[:, TT * c:TT * (c + 1)],
                                    x1s[128 * c:128 * (c + 1), t0:t0 + TT], "DX"))
            r2l = s.dma("sync", rows[0:1, 0:TT], r2s[0:1, t0:t0 + TT], "DX")
            s.wait("vector", r2l)
            rb = vcp(rowbf[0:1, 0:TT], rows[0:1, 0:TT], inc="SV")
            s.wait("tensor", rb, *wrdy)
            bm = mm(psA[0:128, 0:TT], conesr[:], rowbf[0:1, 0:TT], True, True, inc="ST")
            s.wait("scalar", bm)
            r2e = act(s1t[:], psA[0:128, 0:TT], AF.Copy)
            s.wait("vector", r2e)
            for c in range(DC):
                s.wait("vector", x1toks[c])
                s.do("vector", lambda e, c=c: e.scalar_tensor_tensor(
                    out=h2t[:, TT * c:TT * (c + 1)], in0=x1r[:, TT * c:TT * (c + 1)],
                    scalar=gf[:, c:c + 1], in1=s1t[:], op0=ALU.mult, op1=ALU.mult),
                    inc="SV", r=[x1r, s1t, gf], w=[h2t])
            h2tok = s.tok("SV")
            s.wait("tensor", h2tok)
            ev2 = []
            for oc in range(NFC if _P2STOP != "h2" else 0):
                cw = FCW[oc]
                pb1 = psA if oc % 2 == 0 else psB
                pb2 = psO if oc % 2 == 0 else psK
                if oc >= 2:
                    s.wait("tensor", ev2[oc - 2])
                for c in range(DC):
                    mm(pb1[0:cw, 0:TT], w1b[:, DFF * c + 128 * oc:DFF * c + 128 * oc + cw],
                       h2t[:, TT * c:TT * (c + 1)], c == 0, c == DC - 1,
                       inc="ST" if c == DC - 1 else None)
                for c in range(DC):
                    mm(pb2[0:cw, 0:TT], w2b[:, DFF * c + 128 * oc:DFF * c + 128 * oc + cw],
                       h2t[:, TT * c:TT * (c + 1)], c == 0, c == DC - 1,
                       inc="ST" if c == DC - 1 else None)
                umm = s.tok("ST")
                s.wait("scalar", umm)
                if oc >= 1:
                    s.wait("scalar", ev2[oc - 1])
                sl = act(s1t[0:cw, 0:TT], pb1[0:cw, 0:TT], AF.Silu)
                s.wait("vector", sl)
                tk = s.do("vector", lambda e, oc=oc, cw=cw, pb2=pb2: e.tensor_tensor(
                    out=sgl[0:cw, TT * oc:TT * (oc + 1)],
                    in0=s1t[0:cw, 0:TT],
                    in1=pb2[0:cw, 0:TT], op=ALU.mult), inc="SV", r=[s1t, pb2], w=[sgl])
                ev2.append(tk)
            sgl_tok = s.tok("SV")
            ptk = None
            tptk = None
            outw = None
            for sub in range(4):
                if outw is not None:
                    s.wait("sync", tptk)
                xl = s.dma("sync", xk[:], xtok[t0 + 128 * sub:t0 + 128 * (sub + 1), :], "DX")
                s.wait("tensor", sgl_tok)
                if ptk is not None:
                    s.wait("tensor", ptk)
                for c in range(NFC):
                    cw = FCW[c]
                    mm(psW[0:128, 0:512],
                       sgl[0:cw, TT * c + 128 * sub:TT * c + 128 * (sub + 1)],
                       w3b[0:cw, D * c:D * c + 512], c == 0, c == NFC - 1)
                    mm(psW[0:128, 512:1024],
                       sgl[0:cw, TT * c + 128 * sub:TT * c + 128 * (sub + 1)],
                       w3b[0:cw, D * c + 512:D * (c + 1)], c == 0, c == NFC - 1,
                       inc="ST" if c == NFC - 1 else None)
                w3mm = s.tok("ST")
                if tptk is not None:
                    s.wait("tensor", tptk)
                for c in range(DC):
                    tpose(psT[:, 128 * c:128 * (c + 1)],
                          x1r[:, TT * c + 128 * sub:TT * c + 128 * (sub + 1)],
                          inc="ST" if c == DC - 1 else None)
                tpm = s.tok("ST")
                s.wait("vector", tpm, xl)
                if outw is not None:
                    s.wait("vector", outw)
                tptk = vtt(outb[:], xk[:], psT[:, 0:1024], ALU.add, inc="SV")
                s.wait("vector", w3mm)
                ptk = vtt(outb[:], outb[:], psW[0:128, 0:1024], ALU.add, inc="SV")
                s.wait("gpsimd", ptk)
                outw = s.dma("gpsimd", out_d[t0 + 128 * sub:t0 + 128 * (sub + 1), :],
                             outb[:], "DG")
        ph2_end = s.all_toks()
        ph2.close()
    else:
        ph2_end = s.all_toks()

    if 3 in _PHASES:
        ph3 = contextlib.ExitStack()

        def sb3(name, shape, dt):
            return ph3.enter_context(nc.sbuf_tensor(name, shape, dt))

        f1q = sb3("f1q", [128, DC * H4], FP8)
        f2q = sb3("f2q", [128, H4C * D], FP8)
        wst3 = sb3("wst3", [128, H4 // 2], F32)
        xq8r = sb3("xq8r", [128, TT * DC], FP8)
        h16t = sb3("h16t", [128, 2 * TT], FP16)
        hq8 = sb3("hq8", [128, H4C * TT], FP8)
        mk = sb3("mk", [128, 2 * 1024], BF16)
        pbf = sb3("pbf", [128, 2 * 1024], F32)
        y16 = sb3("y16", [128, 1024], FP16)
        ytmp = sb3("ytmp", [128, 1024], F32)
        otb = sb3("otb", [128, 2 * 1024], F32)
        s.barrier(ph2_end)
        w3t = []
        wi3 = 0
        for c in range(DC):
            for hf in range(2):
                if wi3 >= 1:
                    s.wait("sync", w3t[wi3 - 1])
                ld = s.dma("sync", wst3[:],
                           dr["fw1t"][128 * c:128 * (c + 1), 2048 * hf:2048 * (hf + 1)], "DX")
                s.wait("scalar", ld)
                w3t.append(act(f1q[:, H4 * c + 2048 * hf:H4 * c + 2048 * (hf + 1)],
                               wst3[:], AF.Copy))
                wi3 += 1
        for c in range(H4C):
            if wi3 >= 1:
                s.wait("sync", w3t[wi3 - 1])
            ld = s.dma("sync", wst3[:, 0:D], dr["fw2t"][128 * c:128 * (c + 1), :], "DX")
            s.wait("scalar", ld)
            w3t.append(act(f2q[:, D * c:D * (c + 1)], wst3[:, 0:D], AF.Copy))
            wi3 += 1
        wrdy3 = s.tok("SA")
        for i in range(NT):
            t0 = i * TT
            if i > 0:
                s.wait("sync", s.tok("ST"))
                s.wait("scalar", s.tok("SV"))
            xqt = []
            for c in range(DC):
                xqt.append(s.dma("sync", xq8r[:, TT * c:TT * (c + 1)],
                                 xq8s[128 * c:128 * (c + 1), t0:t0 + TT], "DX"))
            s.wait("tensor", wrdy3, *xqt)
            hev = []
            for oc in range(H4C):
                pban = psA if oc % 2 == 0 else psB
                if oc >= 2:
                    s.wait("tensor", hev[oc - 2])
                for c in range(DC):
                    mm(pban[0:128, 0:TT], f1q[:, H4 * c + 128 * oc:H4 * c + 128 * (oc + 1)],
                       xq8r[:, TT * c:TT * (c + 1)], c == 0, c == DC - 1,
                       inc="ST" if c == DC - 1 else None)
                m1t = s.tok("ST")
                s.wait("scalar", m1t)
                act(h16t[:, TT * (oc % 2):TT * (oc % 2 + 1)], pban[0:128, 0:TT],
                    AF.Identity, bias=b1q[:, oc:oc + 1])
                hev.append(act(hq8[:, TT * oc:TT * (oc + 1)],
                               h16t[:, TT * (oc % 2):TT * (oc % 2 + 1)], AF.Relu))
            hq8_tok = s.tok("SA")
            ftk = None
            fhist = []
            ohist = []
            for sub in range(4):
                pp = sub % 2
                if sub >= 2:
                    s.wait("sync", fhist[sub - 2])
                ml = s.dma("sync", mk[:, 1024 * pp:1024 * (pp + 1)],
                           mask_d[t0 + 128 * sub:t0 + 128 * (sub + 1), :], "DX")
                pl = s.dma("sync", pbf[:, 1024 * pp:1024 * (pp + 1)],
                           out_d[t0 + 128 * sub:t0 + 128 * (sub + 1), :], "DX")
                s.wait("tensor", hq8_tok)
                if ftk is not None:
                    s.wait("tensor", ftk, yevp)
                for c in range(H4C):
                    mm(psW[0:128, 0:512],
                       hq8[:, TT * c + 128 * sub:TT * c + 128 * (sub + 1)],
                       f2q[:, D * c:D * c + 512], c == 0, c == H4C - 1)
                    mm(psW[0:128, 512:1024],
                       hq8[:, TT * c + 128 * sub:TT * c + 128 * (sub + 1)],
                       f2q[:, D * c + 512:D * (c + 1)], c == 0, c == H4C - 1,
                       inc="ST" if c == H4C - 1 else None)
                m2t = s.tok("ST")
                s.wait("scalar", m2t)
                if ftk is not None:
                    s.wait("scalar", ftk)
                yev = act(ytmp[:], psW[0:128, 0:1024], AF.Copy)
                s.wait("vector", yev, ml, pl)
                if sub >= 2:
                    s.wait("vector", ohist[sub - 2])
                vtt(y16[:], ytmp[:], b2tile[:], ALU.add)
                vtt(otb[:, 1024 * pp:1024 * (pp + 1)], y16[:],
                    mk[:, 1024 * pp:1024 * (pp + 1)], ALU.mult)
                ftk = vtt(otb[:, 1024 * pp:1024 * (pp + 1)],
                          otb[:, 1024 * pp:1024 * (pp + 1)],
                          pbf[:, 1024 * pp:1024 * (pp + 1)], ALU.add, inc="SV")
                fhist.append(ftk)
                yevp = yev
                s.wait("gpsimd", ftk)
                ohist.append(s.dma("gpsimd", out_d[t0 + 128 * sub:t0 + 128 * (sub + 1), :],
                                   otb[:, 1024 * pp:1024 * (pp + 1)], "DG"))
        ph3.close()

    s.wait("gpsimd", s.tok("DG"))
    s.wait("sync", s.tok("DX"))

    block = es.enter_context(nc.Block())
    s.realize(block)
    es.close()
    return nc


def _consts():
    cid = np.eye(128, dtype=BF)
    cmask = np.fromfunction(lambda r, c: ((r % 64) <= (c % 64)), (128, 128)).astype(BF)
    cob = np.fromfunction(lambda r, c: ((r // 64) == (c // 64)), (128, 128)).astype(BF)
    cones = np.ones((128, 1), BF)
    conesr = np.ones((1, 128), BF)
    cstart = np.zeros((128, TT), BF)
    cstart[:, ::C] = 1
    cbig = np.full((128, TT), 1e36, BF)
    cbig[:, ::C] = 0
    return cid, cmask, cob, cones, conesr, cstart, cbig


def kernel(L0_attn_x, L0_attn_Wq, L0_attn_Wk, L0_attn_Wv, L0_attn_Wg, L0_attn_Wo,
           L0_attn_w1, L0_attn_w2, L0_attn_w3, L0_attn_g_attn, L0_attn_g_ffn,
           L0_ffn_w1, L0_ffn_b1, L0_ffn_w2, L0_ffn_b2):
    x = np.asarray(L0_attn_x, np.float32)
    B = x.shape[0]
    maskml = _dropout_maskmul((B, T, D))
    cid, cmask, cob, cones, conesr, cstart, cbig = _consts()
    shared = {
        "wq": np.ascontiguousarray(L0_attn_Wq, dtype=np.float32),
        "wk": np.ascontiguousarray(L0_attn_Wk, dtype=np.float32),
        "wv": np.ascontiguousarray(L0_attn_Wv, dtype=np.float32),
        "wg": np.ascontiguousarray(L0_attn_Wg, dtype=np.float32),
        "wo": np.ascontiguousarray(L0_attn_Wo, dtype=np.float32),
        "w1a": np.ascontiguousarray(L0_attn_w1, dtype=np.float32),
        "w2a": np.ascontiguousarray(L0_attn_w2, dtype=np.float32),
        "w3a": np.ascontiguousarray(L0_attn_w3, dtype=np.float32),
        "fw1t": np.ascontiguousarray(np.asarray(L0_ffn_w1, np.float32).T),
        "fw2t": np.ascontiguousarray(np.asarray(L0_ffn_w2, np.float32).T),
        "ga": np.ascontiguousarray(np.asarray(L0_attn_g_attn, np.float32).reshape(DC, 128).T),
        "gf": np.ascontiguousarray(np.asarray(L0_attn_g_ffn, np.float32).reshape(DC, 128).T),
        "fb1": np.ascontiguousarray(np.asarray(L0_ffn_b1, np.float32).reshape(H4C, 128).T),
        "fb2": np.ascontiguousarray(np.asarray(L0_ffn_b2, np.float32).reshape(1, D)),
        "cid": cid, "cmask": cmask, "cob": cob, "cones": cones, "conesr": conesr,
        "cstart": cstart, "cbig": cbig, "conesrf": np.ones((1, 128), np.float32),
        "ceps": np.array([[1e-5, 1e-6]], np.float32),
    }
    nc = build_program()
    in_maps = []
    for b in range(B):
        m = dict(shared)
        m["xfm"] = np.ascontiguousarray(x[b].T)
        m["xtok"] = np.ascontiguousarray(x[b])
        m["maskml"] = np.ascontiguousarray(maskml[b])
        in_maps.append(m)
    res = run_bass_kernel_spmd(nc, in_maps, core_ids=list(range(B)))
    return np.stack([np.asarray(r["out"], np.float32) for r in res.results], axis=0)


# version 07
# speedup vs baseline: 1.1579x; 1.1570x over previous
import contextlib

import numpy as np
import ml_dtypes

import concourse.bass as bass
import concourse.mybir as mybir
from concourse.bass_utils import run_bass_kernel_spmd

F32 = mybir.dt.float32
BF16 = mybir.dt.bfloat16
FP16 = mybir.dt.float16
FP8 = mybir.dt.float8e4
AF = mybir.ActivationFunctionType
ALU = mybir.AluOpType
BF = ml_dtypes.bfloat16
E4 = ml_dtypes.float8_e4m3

T, D, NH, DK = 4096, 1024, 16, 64
TT = 512
NT = T // TT
C = 64
NCH = TT // C
NP = 8
DC = 8
DFF = 2736
NFC = 22
FCW = [128] * 21 + [48]
H4 = 4096
H4C = 32

ENGS = ("sync", "tensor", "vector", "scalar", "gpsimd")

_PHASES = (1, 2, 3)
_P1STOP = None
_SCANPART = 'full'
_P2STOP = None
_NT = NT
_DBG = {}


def _threefry2x32(k1, k2, x0, x1):
    rot = [np.array([13, 15, 26, 6], np.uint32), np.array([17, 29, 16, 24], np.uint32)]
    ks = [np.uint32(k1), np.uint32(k2), np.uint32(k1) ^ np.uint32(k2) ^ np.uint32(0x1BD11BDA)]
    x = [x0.astype(np.uint32).copy(), x1.astype(np.uint32).copy()]
    x[0] += ks[0]
    x[1] += ks[1]

    def r_(v, d):
        d = np.uint32(d)
        return (v << d) | (v >> np.uint32(32 - d))

    def rounds(x, rs):
        for rr in rs:
            x[0] += x[1]
            x[1] = r_(x[1], rr)
            x[1] ^= x[0]
        return x

    x = rounds(x, rot[0]); x[0] += ks[1]; x[1] += ks[2] + np.uint32(1)
    x = rounds(x, rot[1]); x[0] += ks[2]; x[1] += ks[0] + np.uint32(2)
    x = rounds(x, rot[0]); x[0] += ks[0]; x[1] += ks[1] + np.uint32(3)
    x = rounds(x, rot[1]); x[0] += ks[1]; x[1] += ks[2] + np.uint32(4)
    x = rounds(x, rot[0]); x[0] += ks[2]; x[1] += ks[0] + np.uint32(5)
    return x[0], x[1]


def _dropout_maskmul(shape, p=0.8, seed=42):
    size = int(np.prod(shape))
    idx = np.arange(size, dtype=np.uint64)
    c1 = (idx >> np.uint64(32)).astype(np.uint32)
    c2 = (idx & np.uint64(0xFFFFFFFF)).astype(np.uint32)
    b1, b2 = _threefry2x32(np.uint32(0), np.uint32(seed), c1, c2)
    bits = b1 ^ b2
    fb = (bits >> np.uint32(9)) | np.uint32(0x3F800000)
    u = fb.view(np.float32) - np.float32(1.0)
    keep = (u < np.float32(p)).reshape(shape)
    return np.where(keep, np.float32(1.25), np.float32(0.0)).astype(BF)


class Sched:
    def __init__(self, nc, es, semnames):
        self.nc = nc
        self.ops = {e: [] for e in ENGS}
        self.sem = {}
        self.cnt = {}
        self._vw = [set(), set()]
        for n in semnames:
            self.sem[n] = es.enter_context(nc.semaphore(n))
            self.cnt[n] = 0

    def tok(self, sem):
        return (sem, self.cnt[sem])

    def wait(self, eng, *xs):
        for x in xs:
            if isinstance(x, tuple):
                sm, c = x
            else:
                sm, c = x, self.cnt[x]
            if c > 0:
                self.ops[eng].append(("w", sm, c))

    @staticmethod
    def _names(aps):
        out = set()
        for a in aps:
            t = getattr(a, "tensor", a)
            n = getattr(t, "name", None)
            if n is not None:
                out.add(n)
        return out

    def do(self, eng, fn, inc=None, n=1, r=(), w=()):
        if eng == "vector":
            rd = self._names(r)
            if rd & (self._vw[0] | self._vw[1]):
                self.ops[eng].append(("d",))
                self._vw = [set(), set()]
            self._vw = [self._names(w), self._vw[0]]
        if inc is not None:
            self.cnt[inc] += n
        self.ops[eng].append(("o", fn, inc, n))
        return self.tok(inc) if inc is not None else None

    def dma(self, eng, out, in_, inc):
        return self.do(eng, lambda e, o=out, i=in_: e.dma_start(out=o, in_=i), inc, 16)

    def barrier(self, toks):
        for e in ENGS:
            self.wait(e, *toks)

    def all_toks(self):
        return [self.tok(x) for x in ("ST", "SV", "SA", "DG", "DX")]

    def realize(self, block):
        def make(name):
            items = self.ops[name]

            def run(eng):
                floor = {}
                for it in items:
                    if it[0] == "w":
                        _, sm, c = it
                        if floor.get(sm, 0) >= c:
                            continue
                        floor[sm] = c
                        eng.wait_ge(self.sem[sm], c)
                    elif it[0] == "d":
                        eng.drain()
                    else:
                        _, fn, inc, n = it
                        ins = fn(eng)
                        if inc is not None:
                            ins.then_inc(self.sem[inc], n)
            return run

        for e in ENGS:
            getattr(block, e)(make(e))


def build_program():
    nc = bass.Bass("TRN2", target_bir_lowering=False, debug=False,
                   detect_race_conditions=False)
    dr = {}

    def din(name, shape, dt):
        dr[name] = nc.dram_tensor(name, shape, dt, kind="ExternalInput").ap()
        return dr[name]

    xfm = din("xfm", [D, T], F32)
    xtok = din("xtok", [T, D], F32)
    for wn in ("wq", "wk", "wv", "wg", "wo"):
        din(wn, [D, D], F32)
    din("w1a", [D, DFF], F32)
    din("w2a", [D, DFF], F32)
    din("w3a", [DFF, D], F32)
    ga_d = din("ga", [128, DC], F32)
    gf_d = din("gf", [128, DC], F32)
    din("fw1t", [D, H4], F32)
    din("fw2t", [H4, D], F32)
    fb1_d = din("fb1", [128, H4C], F32)
    fb2_d = din("fb2", [1, D], F32)
    mask_d = din("maskml", [T, D], BF16)
    cid_d = din("cid", [128, 128], BF16)
    cmask_d = din("cmask", [128, 128], BF16)
    cob_d = din("cob", [128, 128], BF16)
    cones_d = din("cones", [128, 1], BF16)
    conesr_d = din("conesr", [1, 128], BF16)
    conesrf_d = din("conesrf", [1, 128], F32)
    cstart_d = din("cstart", [128, TT], BF16)
    cbig_d = din("cbig", [128, TT], BF16)
    ceps_d = din("ceps", [1, 2], F32)

    x1s = nc.dram_tensor("x1s", [D, T], BF16, kind="Internal").ap()
    xq8s = nc.dram_tensor("xq8s", [D, T], FP8, kind="Internal").ap()
    r2s = nc.dram_tensor("r2s", [1, T], F32, kind="Internal").ap()
    out_d = nc.dram_tensor("out", [T, D], F32, kind="ExternalOutput").ap()
    dbg_d = {}
    for nm in ("dbg_q", "dbg_x1", "dbg_h", "dbg_o"):
        if _DBG.get(nm):
            dbg_d[nm] = nc.dram_tensor(nm, [D, T], BF16, kind="ExternalOutput").ap()
    if _DBG.get("dbg_rows"):
        dbg_d["dbg_rows"] = nc.dram_tensor("dbg_rows", [128, 2048], BF16,
                                           kind="ExternalOutput").ap()
    if _DBG.get("dbg_rowsf"):
        dbg_d["dbg_rowsf"] = nc.dram_tensor("dbg_rowsf", [2, 2048], F32,
                                            kind="ExternalOutput").ap()

    es = contextlib.ExitStack()

    def sb(name, shape, dt):
        return es.enter_context(nc.sbuf_tensor(name, shape, dt))

    cid = sb("cid_s", [128, 128], BF16)
    cmask = sb("cmask_s", [128, 128], BF16)
    cob = sb("cob_s", [128, 128], BF16)
    cones = sb("cones_s", [128, 1], BF16)
    conesr = sb("conesr_s", [1, 128], BF16)
    conesrf = sb("conesrf_s", [1, 128], F32)
    cstart = sb("cstart_s", [128, TT], BF16)
    cbig = sb("cbig_s", [128, TT], BF16)
    ceps = sb("ceps_s", [1, 2], F32)
    ga = sb("ga_s", [128, DC], F32)
    gf = sb("gf_s", [128, DC], F32)
    b1q = sb("b1q", [128, H4C], F32)
    b1q8 = sb("b1q8", [128, H4C], FP8)
    b2row8 = sb("b2row8", [1, D], FP8)
    b2rowb = sb("b2rowb", [1, D], BF16)
    b2tile = sb("b2tile", [128, D], BF16)
    Sst = sb("Sst", [128, 1024], BF16)
    lamC = sb("lamC", [128, 64], F32)
    rows = sb("rows", [1, 2048], F32)
    rowbf = sb("rowbf", [1, 1024], BF16)
    

    psA = es.enter_context(nc.psum_tensor("psA", [128, 512], F32))
    psB = es.enter_context(nc.psum_tensor("psB", [128, 512], F32))
    psO = es.enter_context(nc.psum_tensor("psO", [128, 512], F32))
    psK = es.enter_context(nc.psum_tensor("psK", [128, 512], F32))
    psW = es.enter_context(nc.psum_tensor("psW", [128, 1024], F32))
    psT = es.enter_context(nc.psum_tensor("psT", [128, 1024], BF16))
    psS = es.enter_context(nc.psum_tensor("psS", [128, 512], F32))

    s = Sched(nc, es, ["DX", "DG", "ST", "SV", "SA", "SG"])

    def mm(psap, lhs, rhs, start, stop, tp=None, inc=None):
        skip = getattr(psap, "offset", 0) >= 4096
        return s.do("tensor", lambda e, o=psap, l=lhs, rr=rhs, st_=start, sp=stop, t=tp,
                    sk=skip:
                    e.matmul(o, l, rr, start=st_, stop=sp, tile_position=t,
                             skip_group_check=sk), inc=inc)

    def tpose(psap, inap, inc=None):
        def f(e, o=psap, i=inap):
            with nc.allow_low_precision(reason="bf16 transpose to bf16 psum"):
                return e.transpose(o, i, cid[:])
        return s.do("tensor", f, inc=inc)

    def act(out, in_, func, bias=0.0, scale=1.0, inc="SA"):
        if isinstance(bias, float) and bias != 0.0:
            bias = ceps[0:1, 0:1] if bias == 1e-5 else ceps[0:1, 1:2]
        return s.do("scalar", lambda e, o=out, i=in_, f=func, b=bias, sc=scale:
                    e.activation(out=o, in_=i, func=f, bias=b, scale=sc), inc=inc)

    def vtt(out, in0, in1, op, inc=None, rn=(), wn=()):
        return s.do("vector", lambda e, o=out, a=in0, b=in1, p_=op:
                    e.tensor_tensor(out=o, in0=a, in1=b, op=p_), inc=inc,
                    r=list(rn) + [in0, in1], w=list(wn) + [out])

    def vrecip(out, in_, inc=None, rn=(), wn=()):
        def f(e, o=out, i=in_):
            with nc.allow_low_precision(reason="bf16 reciprocal by design"):
                return e.reciprocal(out=o, in_=i)
        return s.do("vector", f, inc=inc, r=list(rn) + [in_], w=list(wn) + [out])

    def vcp(out, in_, inc=None):
        return s.do("vector", lambda e, o=out, i=in_: e.tensor_copy(out=o, in_=i),
                    inc=inc, r=[in_], w=[out])

    for dsrc, tgt in ((cid_d, cid), (cmask_d, cmask), (cob_d, cob), (cones_d, cones),
                      (conesr_d, conesr), (conesrf_d, conesrf), (cstart_d, cstart), (cbig_d, cbig),
                      (ceps_d, ceps), (ga_d, ga), (gf_d, gf)):
        s.dma("sync", tgt[:], dsrc, "DX")
    s.dma("sync", b1q[:], fb1_d, "DX")
    s.dma("sync", rows[0:1, 1024:1024 + D], fb2_d, "DX")
    c0 = s.tok("DX")
    s.wait("scalar", c0)
    act(b1q8[:], b1q[:], AF.Copy)
    act(b1q[:], b1q8[:], AF.Copy)
    act(b2row8[:], rows[0:1, 1024:1024 + D], AF.Copy)
    tb2 = act(b2rowb[:], b2row8[:], AF.Copy)
    s.wait("tensor", tb2)
    mm(psW[0:128, 0:512], conesr[:], b2rowb[0:1, 0:512], True, True)
    t1 = mm(psW[0:128, 512:1024], conesr[:], b2rowb[0:1, 512:1024], True, True, inc="ST")
    s.wait("scalar", t1)
    b2done = act(b2tile[:], psW[0:128, 0:1024], AF.Copy)

    ph1 = contextlib.ExitStack()

    def sb1(name, shape, dt):
        return ph1.enter_context(nc.sbuf_tensor(name, shape, dt))

    wqb = sb1("wqb", [128, DC * D], BF16)
    wkb = sb1("wkb", [128, DC * D], BF16)
    wvb = sb1("wvb", [128, DC * D], BF16)
    wgb = sb1("wgb", [128, DC * D], BF16)
    wob = sb1("wob", [128, DC * D], BF16)
    wst = sb1("wst", [128, 2 * D], F32)
    wtoks = []
    wi = 0
    for wname, wtgt in (("wq", wqb), ("wk", wkb), ("wg", wgb), ("wv", wvb), ("wo", wob)):
        for c in range(DC):
            sl = D * (wi % 2)
            if wi >= 2:
                s.wait("sync", wtoks[wi - 2])
            ld = s.dma("sync", wst[:, sl:sl + D], dr[wname][128 * c:128 * (c + 1), :], "DX")
            if wi % 2 == 0:
                s.wait("vector", ld)
                tk = vcp(wtgt[:, D * c:D * (c + 1)], wst[:, sl:sl + D], inc="SV")
            else:
                s.wait("scalar", ld)
                tk = act(wtgt[:, D * c:D * (c + 1)], wst[:, sl:sl + D], AF.Copy)
            wtoks.append(tk)
            wi += 1
    w_ready = [s.tok("SV"), s.tok("SA")]

    xfp = sb1("xfp", [128, 2 * TT], F32)
    xbf = sb1("xbf", [128, TT * DC], BF16)
    xsq = sb1("xsq", [128, 2 * TT], BF16)
    tmp1 = sb1("tmp1", [128, 2 * TT], F32)
    xq8t = sb1("xq8t", [128, 2 * TT], FP8)
    xnb = sb1("xnb", [128, TT * DC], BF16)
    ht = sb1("ht", [128, TT * DC], BF16)
    qb = sb1("qb", [128, 2 * TT], BF16)
    kb = sb1("kb", [128, 2 * TT], BF16)
    ab = sb1("ab", [128, 2 * TT], BF16)
    iat = sb1("iat", [128, TT], BF16)
    d1a = sb1("d1a", [128, TT], BF16)
    d1b = sb1("d1b", [128, TT], BF16)
    lam = sb1("lam", [128, TT], BF16)
    ilam = sb1("ilam", [128, TT], BF16)
    sqt = sb1("sqt", [128, TT], BF16)
    dqM = sb1("dqM", [128, TT], BF16)
    dkM = sb1("dkM", [128, TT], BF16)
    khf = sb1("khf", [128, TT], BF16)
    Qt = sb1("Qt", [128, TT * NP], BF16)
    Kt = sb1("Kt", [128, TT * NP], BF16)
    Khtok = sb1("Khtok", [128, TT * DC], BF16)
    Vtok = sb1("Vtok", [128, TT * DC], BF16)
    ptl = sb1("ptl", [128, 256], BF16)
    Ofm = sb1("Ofm", [128, TT * NP], BF16)
    x1t = sb1("x1t", [128, TT * DC], BF16)

    s.do("vector", lambda e: e.memset(Sst[:], 0.0), inc="SV", w=[Sst])
    s.barrier(w_ready + [s.tok("SA")])

    mask_toks = {}
    upd_toks = {}

    def emit_cell_head(g, ch, p, ci, prev):
        par = ch % 2
        b = 64 * par
        pr = slice(b, b + 64)
        tc = slice(TT * p + C * ch, TT * p + C * (ch + 1))
        pc = 64 * (ci % 2)
        if prev is not None:
            s.wait("tensor", mask_toks[(prev[0], prev[2])])
        mm(psW[pr, pc:pc + 64], Kt[0:64, tc], Qt[0:64, tc], True, True, (0, b))
        return mm(psW[pr, 512 + pc:512 + pc + 64], Kt[64:128, tc], Qt[64:128, tc],
                  True, True, (64, b), inc="ST")

    def emit_cell_mask(g, ch, p, ci, pmm):
        par = ch % 2
        b = 64 * par
        pr = slice(b, b + 64)
        pc = 64 * (ci % 2)
        ps_ = 128 * (ci % 2)
        s.wait("vector", pmm)
        mask_toks[(g, p)] = s.do(
            "vector", lambda e, pr=pr, pc=pc, ps_=ps_: e.tensor_tensor(
                out=ptl[pr, ps_:ps_ + 128].rearrange("q (a x) -> q a x", a=2),
                in0=psW[pr, 0:1024].rearrange("q (a x) -> q a x", a=2)[:, :, pc:pc + 64],
                in1=cmask[pr, 0:128].rearrange("q (a x) -> q a x", a=2),
                op=ALU.mult),
            inc="SV", r=[psW, cmask], w=[ptl])

    def emit_cell_tail(i, g, ch, p, ci):
        if _SCANPART == "head":
            return
        par = ch % 2
        b = 64 * par
        pr = slice(b, b + 64)
        tc = slice(TT * p + C * ch, TT * p + C * (ch + 1))
        sps = 512 * (g % 2)
        spd = 512 * ((g + 1) % 2)
        vbase = 1024 * (ch // 2) + 128 * p
        pc = 64 * (ci % 2)
        ps_ = 128 * (ci % 2)
        s.wait("tensor", mask_toks[(g, p)])
        mm(psO[0:64, pc:pc + 64], Vtok[pr, vbase:vbase + 64], ptl[pr, ps_:ps_ + 64],
           True, True, (b, 0))
        mm(psO[64:128, pc:pc + 64], Vtok[pr, vbase + 64:vbase + 128],
           ptl[pr, ps_ + 64:ps_ + 128], True, True, (b, 64))
        mm(psK[0:64, pc:pc + 64], Khtok[pr, vbase:vbase + 64], Vtok[pr, vbase:vbase + 64],
           True, True, (b, 0))
        kvm = mm(psK[64:128, pc:pc + 64], Khtok[pr, vbase + 64:vbase + 128],
                 Vtok[pr, vbase + 64:vbase + 128], True, True, (b, 64), inc="ST")
        if (g - 1, p) in upd_toks:
            s.wait("tensor", upd_toks[(g - 1, p)])
        mm(psS[0:64, pc:pc + 64], Sst[0:64, sps + 64 * p:sps + 64 * (p + 1)],
           Qt[0:64, tc], True, True, (0, 0))
        qsm = mm(psS[64:128, pc:pc + 64], Sst[64:128, sps + 64 * p:sps + 64 * (p + 1)],
                 Qt[64:128, tc], True, True, (64, 64), inc="ST")
        s.wait("scalar", qsm)
        ocp = s.do("scalar", lambda e, tc=tc, pc=pc: e.activation(
            out=Ofm[:, tc], in_=psO[:, pc:pc + 64], func=AF.Copy, bias=0.0, scale=1.0),
            inc="SA")
        s.wait("vector", ocp)
        s.do("vector", lambda e, tc=tc, pc=pc: e.tensor_tensor(
            out=Ofm[:, tc], in0=Ofm[:, tc], in1=psS[:, pc:pc + 64], op=ALU.add),
            inc="SV", r=[Ofm, psS], w=[Ofm])
        s.wait("vector", kvm)
        upd_toks[(g, p)] = s.do(
            "vector", lambda e, p=p, ch=ch, sps=sps, spd=spd, pc=pc: e.scalar_tensor_tensor(
                out=Sst[:, spd + 64 * p:spd + 64 * (p + 1)],
                in0=Sst[:, sps + 64 * p:sps + 64 * (p + 1)],
                scalar=lamC[:, 8 * p + ch:8 * p + ch + 1],
                in1=psK[:, pc:pc + 64], op0=ALU.mult, op1=ALU.add),
            inc="SV", r=[Sst, lamC, psK], w=[Sst])

    def emit_p1_tile(i, prev_toks):
        t0 = i * TT
        if prev_toks is not None:
            s.wait("vector", prev_toks[0], prev_toks[2])
            s.wait("scalar", prev_toks[0])
            s.wait("sync", prev_toks[1])
            s.wait("tensor", prev_toks[1])
        xtoks = []
        cast_toks = []
        sum_toks = []
        for c in range(DC):
            pp = c % 2
            if c >= 2:
                s.wait("sync", cast_toks[c - 2])
            xtoks.append(s.dma("sync", xfp[:, TT * pp:TT * (pp + 1)],
                               xfm[128 * c:128 * (c + 1), t0:t0 + TT], "DX"))
            s.wait("scalar", xtoks[c])
            cast_toks.append(act(xbf[:, TT * c:TT * (c + 1)],
                                 xfp[:, TT * pp:TT * (pp + 1)], AF.Copy))
            if c >= 2:
                s.wait("scalar", sum_toks[c - 2])
            sqk = act(xsq[:, TT * (c % 2):TT * (c % 2 + 1)],
                      xbf[:, TT * c:TT * (c + 1)], AF.Square)
            s.wait("tensor", sqk)
            sum_toks.append(mm(psS[0:1, 0:TT], cones[:],
                               xsq[:, TT * (c % 2):TT * (c % 2 + 1)],
                               c == 0, False if c < DC - 1 else True, inc="ST"))
        sx2 = s.tok("ST")
        s.wait("vector", sx2)
        s.do("vector", lambda e: e.tensor_scalar_mul(
            out=rows[0:1, TT:2 * TT], in0=psS[0:1, 0:TT], scalar1=1.0 / D),
            inc="SV", r=[psS], w=[rows])
        ev1 = s.tok("SV")
        for c in range(DC):
            s.wait("tensor", ev1)
            mm(psS[0:1, 0:TT], cones[:], xbf[:, TT * c:TT * (c + 1)],
               c == 0, c == DC - 1, inc="ST" if c == DC - 1 else None)
        sx1 = s.tok("ST")
        s.wait("vector", sx1)
        mevtk = s.do("vector", lambda e: e.tensor_scalar_mul(
            out=rows[0:1, 0:TT], in0=psS[0:1, 0:TT], scalar1=1.0 / D),
            inc="SV", r=[psS], w=[rows])
        if "dbg_rowsf" in dbg_d and i == 0:
            s.wait("gpsimd", mevtk)
            s.dma("gpsimd", dbg_d["dbg_rowsf"][0:1, :], rows[0:1, :], "DG")
            s.wait("vector", s.tok("DG"))
        vtt(rows[0:1, 2 * TT:3 * TT], rows[0:1, 0:TT], rows[0:1, 0:TT], ALU.mult)
        vrow = vtt(rows[0:1, 2 * TT:3 * TT], rows[0:1, TT:2 * TT],
                   rows[0:1, 2 * TT:3 * TT], ALU.subtract, inc="SV")
        s.wait("scalar", vrow)
        sq1 = act(rows[0:1, 3 * TT:4 * TT], rows[0:1, 2 * TT:3 * TT], AF.Sqrt, bias=1e-5)
        s.wait("vector", sq1)
        s.do("vector", lambda e: e.reciprocal(out=rows[0:1, TT:2 * TT],
                                              in_=rows[0:1, 3 * TT:4 * TT]),
             inc="SV", r=[rows], w=[rows])
        vtt(rows[0:1, 3 * TT:4 * TT], rows[0:1, 0:TT], rows[0:1, TT:2 * TT], ALU.mult)
        vtt(rows[0:1, 0:TT], rows[0:1, 2 * TT:3 * TT], rows[0:1, TT:2 * TT], ALU.mult)
        vs2 = vtt(rows[0:1, 0:TT], rows[0:1, 0:TT], rows[0:1, TT:2 * TT], ALU.mult, inc="SV")
        s.wait("scalar", vs2)
        sq2 = act(rows[0:1, 2 * TT:3 * TT], rows[0:1, 0:TT], AF.Sqrt, bias=1e-6)
        s.wait("vector", sq2)
        s.do("vector", lambda e: e.reciprocal(out=rows[0:1, 0:TT],
                                              in_=rows[0:1, 2 * TT:3 * TT]),
             inc="SV", r=[rows], w=[rows])
        if "dbg_rowsf" in dbg_d and i == 0:
            s.wait("gpsimd", s.tok("SV"))
            s.dma("gpsimd", dbg_d["dbg_rowsf"][1:2, :], rows[0:1, :], "DG")
            s.wait("vector", s.tok("DG"))
        for src0, tgt in ((TT, psA), (3 * TT, psB), (0, psO)):
            vcp(rowbf[0:1, 0:TT], rows[0:1, src0:src0 + TT])
            vtt(rows[0:1, 2 * TT:3 * TT], rows[0:1, src0:src0 + TT],
                rowbf[0:1, 0:TT], ALU.subtract)
            hl = vcp(rowbf[0:1, TT:2 * TT], rows[0:1, 2 * TT:3 * TT], inc="SV")
            s.wait("tensor", hl)
            mm(tgt[0:128, 0:TT], conesr[:], rowbf[0:1, 0:TT], True, False)
            bm = mm(tgt[0:128, 0:TT], conesr[:], rowbf[0:1, TT:2 * TT], False, True,
                    inc="ST")
            s.wait("vector", bm)
        if "dbg_rows" in dbg_d and i == 0:
            s.wait("scalar", bm)
            act(xsq[:, 0:TT], psA[0:128, 0:TT], AF.Copy)
            act(xsq[:, TT:2 * TT], psB[0:128, 0:TT], AF.Copy)
            dr8 = act(xbf[:, 0:TT], psO[0:128, 0:TT], AF.Copy)
            s.wait("gpsimd", dr8)
            s.dma("gpsimd", dbg_d["dbg_rows"][:, 0:TT], xsq[:, 0:TT], "DG")
            s.dma("gpsimd", dbg_d["dbg_rows"][:, TT:2 * TT], xsq[:, TT:2 * TT], "DG")
            s.dma("gpsimd", dbg_d["dbg_rows"][:, 2 * TT:3 * TT], xbf[:, 0:TT], "DG")
            s.dma("gpsimd", dbg_d["dbg_rows"][0:1, 3 * TT:4 * TT], rows[0:1, TT:2 * TT], "DG")
            s.wait("vector", s.tok("DG"))
            s.wait("scalar", s.tok("DG"))
        s.wait("vector", bm)
        xn_sv = []
        q8dma = []
        xdma2 = []
        xnb_acts = []
        for c in range(DC):
            pp = c % 2
            if c >= 2:
                s.wait("sync", xn_sv[c - 2])
            else:
                s.wait("sync", cast_toks[DC - 2 + c])
            xdma2.append(s.dma("sync", xfp[:, TT * pp:TT * (pp + 1)],
                               xfm[128 * c:128 * (c + 1), t0:t0 + TT], "DX"))
            s.wait("vector", xdma2[c])
            if c >= 2:
                s.wait("vector", xnb_acts[c - 2])
            vtt(tmp1[:, TT * pp:TT * (pp + 1)], xfp[:, TT * pp:TT * (pp + 1)],
                psA[0:128, 0:TT], ALU.mult)
            x2t = vtt(tmp1[:, TT * pp:TT * (pp + 1)], tmp1[:, TT * pp:TT * (pp + 1)],
                      psB[0:128, 0:TT], ALU.subtract, inc="SV")
            s.wait("scalar", x2t)
            if 3 in _PHASES:
                if c >= 2 and q8dma[c - 2] is not None:
                    s.wait("scalar", q8dma[c - 2])
                q8 = act(xq8t[:, TT * pp:TT * (pp + 1)], tmp1[:, TT * pp:TT * (pp + 1)],
                         AF.Copy)
                s.wait("gpsimd", q8)
                q8dma.append(s.dma("gpsimd", xq8s[128 * c:128 * (c + 1), t0:t0 + TT],
                                   xq8t[:, TT * pp:TT * (pp + 1)], "DG"))
            xnb_acts.append(act(xnb[:, TT * c:TT * (c + 1)], tmp1[:, TT * pp:TT * (pp + 1)],
                                AF.Copy))
            vtt(xsq[:, TT * pp:TT * (pp + 1)], tmp1[:, TT * pp:TT * (pp + 1)],
                psO[0:128, 0:TT], ALU.mult)
            htt = s.do("vector", lambda e, c=c, pp=pp: e.tensor_scalar(
                out=ht[:, TT * c:TT * (c + 1)], in0=xsq[:, TT * pp:TT * (pp + 1)],
                scalar1=ga[:, c:c + 1], scalar2=None, op0=ALU.mult),
                inc="SV", r=[xsq, ga], w=[ht])
            xn_sv.append(htt)
        htok = s.tok("SV")
        s.wait("tensor", s.tok("SA"))
        if _P1STOP == "ln":
            return [s.tok("ST"), s.tok("SV"), s.tok("DG")]
        evac_toks = {}
        tp_toks = {}
        prep_psO = None
        s.wait("tensor", htok)
        prep_sv = {}
        for p in range(NP):
            pp = p % 2
            pban, poth = (psA, psB) if pp == 0 else (psB, psA)
            if p >= 1:
                s.wait("tensor", evac_toks[p - 1])
            if p >= 2:
                s.wait("tensor", prep_sv[p - 2])
            for c in range(DC):
                mm(pban[0:128, 0:TT], wqb[:, D * c + 128 * p:D * c + 128 * (p + 1)],
                   ht[:, TT * c:TT * (c + 1)], c == 0, c == DC - 1,
                   inc="ST" if c == DC - 1 else None)
            qmm = s.tok("ST")
            for c in range(DC):
                mm(poth[0:128, 0:TT], wkb[:, D * c + 128 * p:D * c + 128 * (p + 1)],
                   ht[:, TT * c:TT * (c + 1)], c == 0, c == DC - 1,
                   inc="ST" if c == DC - 1 else None)
            kmm = s.tok("ST")
            s.wait("scalar", qmm)
            qev = act(qb[:, TT * pp:TT * (pp + 1)], pban[0:128, 0:TT], AF.Copy)
            s.wait("scalar", kmm)
            kev = act(kb[:, TT * pp:TT * (pp + 1)], poth[0:128, 0:TT], AF.Copy)
            s.wait("tensor", qev)
            for c in range(DC):
                mm(pban[0:128, 0:TT], wgb[:, D * c + 128 * p:D * c + 128 * (p + 1)],
                   ht[:, TT * c:TT * (c + 1)], c == 0, c == DC - 1,
                   inc="ST" if c == DC - 1 else None)
            gmm = s.tok("ST")
            s.wait("scalar", gmm)
            gev = act(ab[:, TT * pp:TT * (pp + 1)], pban[0:128, 0:TT], AF.Sigmoid)
            evac_toks[p] = gev
            qsl = qb[:, TT * pp:TT * (pp + 1)]
            ksl = kb[:, TT * pp:TT * (pp + 1)]
            asl = ab[:, TT * pp:TT * (pp + 1)]
            s.wait("scalar", gev)
            sq1p = act(sqt[:], qsl, AF.Square)
            if prep_psO is not None:
                s.wait("tensor", prep_psO)
            s.wait("tensor", sq1p)
            m1 = mm(psO[0:128, 0:TT], cob[:], sqt[:], True, True, inc="ST")
            s.wait("scalar", m1)
            if p >= 1:
                s.wait("scalar", prep_sv[p - 1])
            act(tmp1[:, 0:TT], psO[0:128, 0:TT], AF.Ln)
            dqA = act(dqM[:], tmp1[:, 0:TT], AF.Exp, scale=-0.5)
            sq2p = act(sqt[:], ksl, AF.Square)
            s.wait("tensor", sq2p)
            m2 = mm(psO[0:128, 0:TT], cob[:], sqt[:], True, True, inc="ST")
            s.wait("scalar", m2)
            act(tmp1[:, 0:TT], psO[0:128, 0:TT], AF.Ln)
            dkA = act(dkM[:], tmp1[:, 0:TT], AF.Exp, scale=-0.5)
            prep_psO = s.tok("SA")
            s.wait("vector", gev)
            vtt(d1a[:], asl, cstart[:], ALU.mult, rn=[ab])
            s.do("vector", lambda e, a=asl: e.tensor_tensor_scan(
                out=lam[:], data0=a, data1=d1a[:], initial=1.0,
                op0=ALU.mult, op1=ALU.max), inc="SV", r=[ab, d1a], w=[lam])
            lmt = s.tok("SV")
            s.do("vector", lambda e, p=p: e.tensor_copy(
                out=lamC[:, 8 * p:8 * (p + 1)],
                in_=lam.rearrange("q (c x) -> q c x", x=C)[:, :, C - 1]),
                r=[lam], w=[lamC])
            s.wait("scalar", lmt)
            act(tmp1[:, 0:TT], lam[:], AF.Ln)
            ils = act(ilam[:], tmp1[:, 0:TT], AF.Exp, scale=-1.0)
            s.wait("vector", m2)
            vtt(sqt[:], qsl, lam[:], ALU.mult, rn=[qb])
            s.wait("vector", ils)
            vtt(iat[:], ksl, ilam[:], ALU.mult, rn=[kb])
            vtt(Qt[:, TT * p:TT * (p + 1)], sqt[:], dqM[:], ALU.mult)
            vtt(Kt[:, TT * p:TT * (p + 1)], iat[:], dkM[:], ALU.mult)
            for ch in range(NCH):
                khdone = s.do("vector", lambda e, p=p, ch=ch: e.tensor_scalar(
                    out=khf[:, C * ch:C * (ch + 1)],
                    in0=Kt[:, TT * p + C * ch:TT * p + C * (ch + 1)],
                    scalar1=lamC[:, 8 * p + ch:8 * p + ch + 1], scalar2=None,
                    op0=ALU.mult), inc="SV" if ch == NCH - 1 else None,
                    r=[Kt, lamC], w=[khf])
            prep_sv[p] = khdone
            s.wait("tensor", khdone)
            if p > 0:
                s.wait("tensor", tp_toks[p - 1])
            tb = 512 * (p % 2)
            for sub in range(4):
                tpose(psT[:, tb + 128 * sub:tb + 128 * (sub + 1)],
                      khf[:, 128 * sub:128 * (sub + 1)],
                      inc="ST" if sub == 3 else None)
            tpmm = s.tok("ST")
            s.wait("vector", tpmm)
            tp_toks[p] = s.do("vector", lambda e, p=p, tb=tb: e.tensor_copy(
                out=Khtok.rearrange("q (sx qq x) -> q sx qq x", sx=4, qq=8)[:, :, p, :],
                in_=psT[:, tb:tb + 512].rearrange("q (sx x) -> q sx x", sx=4)),
                inc="SV", r=[psT], w=[Khtok])
        if _P1STOP == "proj":
            return [s.tok("ST"), s.tok("SV"), s.tok("DG")]
        vev = []
        vg = 0
        s.wait("tensor", evac_toks[NP - 1])
        for sub in range(4):
            for hf in range(2):
                pban = psA if vg % 2 == 0 else psB
                if vg >= 2:
                    s.wait("tensor", vev[vg - 2])
                for c in range(DC):
                    mm(pban[0:128, 0:TT],
                       ht[:, TT * c + 128 * sub:TT * c + 128 * (sub + 1)],
                       wvb[:, D * c + 512 * hf:D * c + 512 * (hf + 1)],
                       c == 0, c == DC - 1, inc="ST" if c == DC - 1 else None)
                vmm = s.tok("ST")
                s.wait("vector", vmm)
                tk = s.do("vector", lambda e, sub=sub, hf=hf, pban=pban: e.tensor_copy(
                    out=Vtok[:, 1024 * sub + 512 * hf:1024 * sub + 512 * (hf + 1)],
                    in_=pban[0:128, 0:TT]), inc="SV", r=[pban], w=[Vtok])
                vev.append(tk)
                vg += 1
        s.wait("tensor", s.tok("SV"), s.tok("SA"))
        cells = [(ch, p) for ch in range(NCH) for p in range(NP)]
        prev = None
        for ci, (ch, p) in enumerate(cells):
            g = NCH * i + ch
            pmm = emit_cell_head(g, ch, p, ci, prev)
            if prev is not None:
                pg, pch, ppp, pci = prev
                emit_cell_tail(i, pg, pch, ppp, pci)
            emit_cell_mask(g, ch, p, ci, pmm)
            prev = (g, ch, p, ci)
        if prev is not None:
            emit_cell_tail(i, prev[0], prev[1], prev[2], prev[3])
        if _P1STOP == "scan":
            return [s.tok("ST"), s.tok("SV"), s.tok("DG")]
        sv_now = s.tok("SV")
        s.wait("tensor", sv_now)
        wo_ev = []
        r2sum = []
        for oc in range(DC):
            pban = psA if oc % 2 == 0 else psB
            if oc >= 2:
                s.wait("tensor", wo_ev[oc - 2])
            for p in range(NP):
                mm(pban[0:128, 0:TT], wob[:, D * p + 128 * oc:D * p + 128 * (oc + 1)],
                   Ofm[:, TT * p:TT * (p + 1)], p == 0, p == NP - 1,
                   inc="ST" if p == NP - 1 else None)
            wmm = s.tok("ST")
            s.wait("vector", wmm)
            x1e = vtt(x1t[:, TT * oc:TT * (oc + 1)], pban[0:128, 0:TT],
                      xnb[:, TT * oc:TT * (oc + 1)], ALU.add, inc="SV")
            wo_ev.append(x1e)
            s.wait("scalar", x1e)
            if oc >= 2:
                s.wait("scalar", r2sum[oc - 2])
            xsqe = act(xsq[:, TT * (oc % 2):TT * (oc % 2 + 1)],
                       x1t[:, TT * oc:TT * (oc + 1)], AF.Square)
            s.wait("tensor", xsqe)
            r2sum.append(mm(psS[0:1, 0:TT], cones[:],
                            xsq[:, TT * (oc % 2):TT * (oc % 2 + 1)],
                            oc == 0, oc == DC - 1, inc="ST"))
            if 2 in _PHASES:
                s.wait("gpsimd", x1e)
                s.dma("gpsimd", x1s[128 * oc:128 * (oc + 1), t0:t0 + TT],
                      x1t[:, TT * oc:TT * (oc + 1)], "DG")
        r2mm = s.tok("ST")
        s.wait("scalar", r2mm)
        r2a = act(rows[0:1, 0:TT], psS[0:1, 0:TT], AF.Sqrt, bias=1e-6, scale=1.0 / D)
        s.wait("vector", r2a)
        r2v = s.do("vector", lambda e: e.reciprocal(
            out=rows[0:1, TT:2 * TT], in_=rows[0:1, 0:TT]), inc="SV",
            r=[rows], w=[rows])
        s.wait("gpsimd", r2v)
        s.dma("gpsimd", r2s[0:1, t0:t0 + TT], rows[0:1, TT:2 * TT], "DG")
        for nm, buf in (("dbg_q", Qt), ("dbg_h", ht), ("dbg_o", Ofm), ("dbg_x1", x1t)):
            if nm in dbg_d:
                dd = s.tok("SV")
                s.wait("gpsimd", dd, s.tok("SA"))
                nchunk = buf.shape[1] // TT
                for c in range(min(DC, nchunk)):
                    s.dma("gpsimd", dbg_d[nm][128 * c:128 * (c + 1), t0:t0 + TT],
                          buf[:, TT * c:TT * (c + 1)], "DG")
        return [s.tok("ST"), s.tok("SV"), s.tok("DG")]

    if 1 in _PHASES:
        prev_toks = None
        for i in range(_NT):
            prev_toks = emit_p1_tile(i, prev_toks)

    ph1_end = s.all_toks()
    ph1.close()

    if 2 in _PHASES:
        ph2 = contextlib.ExitStack()

        def sb2(name, shape, dt):
            return ph2.enter_context(nc.sbuf_tensor(name, shape, dt))

        w1b = sb2("w1b", [128, DC * DFF], BF16)
        w2b = sb2("w2b", [128, DC * DFF], BF16)
        w3b = sb2("w3b", [128, NFC * D], BF16)
        wst2 = sb2("wst2", [128, DFF // 2], F32)
        x1r = sb2("x1r", [128, TT * DC], BF16)
        h2t = sb2("h2t", [128, TT * DC], BF16)
        s1t = sb2("s1t", [128, TT], BF16)
        sgl = sb2("sgl", [128, NFC * TT], BF16)
        xk = sb2("xk", [128, 1024], F32)
        outb = sb2("outb", [128, 1024], F32)
        s.barrier(ph1_end)
        w2toks = []
        wi2 = 0
        HF2 = DFF // 2
        for wname, wtgt in (("w1a", w1b), ("w2a", w2b)):
            for c in range(DC):
                for hf in range(2):
                    if wi2 >= 1:
                        s.wait("sync", w2toks[wi2 - 1])
                    ld = s.dma("sync", wst2[:],
                               dr[wname][128 * c:128 * (c + 1), HF2 * hf:HF2 * (hf + 1)], "DX")
                    if wi2 % 2 == 0:
                        s.wait("vector", ld)
                        tk = vcp(wtgt[:, DFF * c + HF2 * hf:DFF * c + HF2 * (hf + 1)],
                                 wst2[:], inc="SV")
                    else:
                        s.wait("scalar", ld)
                        tk = act(wtgt[:, DFF * c + HF2 * hf:DFF * c + HF2 * (hf + 1)],
                                 wst2[:], AF.Copy)
                    w2toks.append(tk)
                    wi2 += 1
        for c in range(NFC):
            cw = FCW[c]
            if wi2 >= 1:
                s.wait("sync", w2toks[wi2 - 1])
            ld = s.dma("sync", wst2[0:cw, 0:D], dr["w3a"][128 * c:128 * c + cw, :], "DX")
            if wi2 % 2 == 0:
                s.wait("vector", ld)
                tk = vcp(w3b[0:cw, D * c:D * (c + 1)], wst2[0:cw, 0:D], inc="SV")
            else:
                s.wait("scalar", ld)
                tk = act(w3b[0:cw, D * c:D * (c + 1)], wst2[0:cw, 0:D], AF.Copy)
            w2toks.append(tk)
            wi2 += 1
        wrdy = [s.tok("SV"), s.tok("SA")]
        for i in range(NT):
            t0 = i * TT
            if i > 0:
                s.wait("sync", s.tok("ST"), s.tok("SV"))
                s.wait("vector", s.tok("ST"))
            x1toks = []
            for c in range(DC):
                x1toks.append(s.dma("sync", x1r[:, TT * c:TT * (c + 1)],
                                    x1s[128 * c:128 * (c + 1), t0:t0 + TT], "DX"))
            r2l = s.dma("sync", rows[0:1, 0:TT], r2s[0:1, t0:t0 + TT], "DX")
            s.wait("vector", r2l)
            rb = vcp(rowbf[0:1, 0:TT], rows[0:1, 0:TT], inc="SV")
            s.wait("tensor", rb, *wrdy)
            bm = mm(psA[0:128, 0:TT], conesr[:], rowbf[0:1, 0:TT], True, True, inc="ST")
            s.wait("scalar", bm)
            r2e = act(s1t[:], psA[0:128, 0:TT], AF.Copy)
            s.wait("vector", r2e)
            for c in range(DC):
                s.wait("vector", x1toks[c])
                s.do("vector", lambda e, c=c: e.scalar_tensor_tensor(
                    out=h2t[:, TT * c:TT * (c + 1)], in0=x1r[:, TT * c:TT * (c + 1)],
                    scalar=gf[:, c:c + 1], in1=s1t[:], op0=ALU.mult, op1=ALU.mult),
                    inc="SV", r=[x1r, s1t, gf], w=[h2t])
            h2tok = s.tok("SV")
            s.wait("tensor", h2tok)
            ev2 = []
            for oc in range(NFC if _P2STOP != "h2" else 0):
                cw = FCW[oc]
                pb1 = psA if oc % 2 == 0 else psB
                pb2 = psO if oc % 2 == 0 else psK
                if oc >= 2:
                    s.wait("tensor", ev2[oc - 2])
                for c in range(DC):
                    mm(pb1[0:cw, 0:TT], w1b[:, DFF * c + 128 * oc:DFF * c + 128 * oc + cw],
                       h2t[:, TT * c:TT * (c + 1)], c == 0, c == DC - 1,
                       inc="ST" if c == DC - 1 else None)
                for c in range(DC):
                    mm(pb2[0:cw, 0:TT], w2b[:, DFF * c + 128 * oc:DFF * c + 128 * oc + cw],
                       h2t[:, TT * c:TT * (c + 1)], c == 0, c == DC - 1,
                       inc="ST" if c == DC - 1 else None)
                umm = s.tok("ST")
                s.wait("scalar", umm)
                if oc >= 1:
                    s.wait("scalar", ev2[oc - 1])
                sl = act(s1t[0:cw, 0:TT], pb1[0:cw, 0:TT], AF.Silu)
                s.wait("vector", sl)
                tk = s.do("vector", lambda e, oc=oc, cw=cw, pb2=pb2: e.tensor_tensor(
                    out=sgl[0:cw, TT * oc:TT * (oc + 1)],
                    in0=s1t[0:cw, 0:TT],
                    in1=pb2[0:cw, 0:TT], op=ALU.mult), inc="SV", r=[s1t, pb2], w=[sgl])
                ev2.append(tk)
            sgl_tok = s.tok("SV")
            ptk = None
            tptk = None
            outw = None
            for sub in range(4):
                if outw is not None:
                    s.wait("sync", tptk)
                xl = s.dma("sync", xk[:], xtok[t0 + 128 * sub:t0 + 128 * (sub + 1), :], "DX")
                s.wait("tensor", sgl_tok)
                if ptk is not None:
                    s.wait("tensor", ptk)
                for c in range(NFC):
                    cw = FCW[c]
                    mm(psW[0:128, 0:512],
                       sgl[0:cw, TT * c + 128 * sub:TT * c + 128 * (sub + 1)],
                       w3b[0:cw, D * c:D * c + 512], c == 0, c == NFC - 1)
                    mm(psW[0:128, 512:1024],
                       sgl[0:cw, TT * c + 128 * sub:TT * c + 128 * (sub + 1)],
                       w3b[0:cw, D * c + 512:D * (c + 1)], c == 0, c == NFC - 1,
                       inc="ST" if c == NFC - 1 else None)
                w3mm = s.tok("ST")
                if tptk is not None:
                    s.wait("tensor", tptk)
                for c in range(DC):
                    tpose(psT[:, 128 * c:128 * (c + 1)],
                          x1r[:, TT * c + 128 * sub:TT * c + 128 * (sub + 1)],
                          inc="ST" if c == DC - 1 else None)
                tpm = s.tok("ST")
                s.wait("vector", tpm, xl)
                if outw is not None:
                    s.wait("vector", outw)
                tptk = vtt(outb[:], xk[:], psT[:, 0:1024], ALU.add, inc="SV")
                s.wait("vector", w3mm)
                ptk = vtt(outb[:], outb[:], psW[0:128, 0:1024], ALU.add, inc="SV")
                s.wait("gpsimd", ptk)
                outw = s.dma("gpsimd", out_d[t0 + 128 * sub:t0 + 128 * (sub + 1), :],
                             outb[:], "DG")
        ph2_end = s.all_toks()
        ph2.close()
    else:
        ph2_end = s.all_toks()

    if 3 in _PHASES:
        ph3 = contextlib.ExitStack()

        def sb3(name, shape, dt):
            return ph3.enter_context(nc.sbuf_tensor(name, shape, dt))

        f1q = sb3("f1q", [128, DC * H4], FP8)
        f2q = sb3("f2q", [128, H4C * D], FP8)
        wst3 = sb3("wst3", [128, H4], F32)
        xq8r = sb3("xq8r", [128, TT * DC], FP8)
        h16t = sb3("h16t", [128, 2 * TT], FP16)
        hq8 = sb3("hq8", [128, H4C * TT], FP8)
        mk = sb3("mk", [128, 2 * 1024], BF16)
        pbf = sb3("pbf", [128, 2 * 1024], F32)
        y16 = sb3("y16", [128, 1024], FP16)
        ytmp = sb3("ytmp", [128, 1024], F32)
        otb = sb3("otb", [128, 2 * 1024], F32)
        s.barrier(ph2_end)
        w3t = []
        wi3 = 0
        for c in range(DC):
            for hf in range(2):
                sl = 2048 * (wi3 % 2)
                if wi3 >= 2:
                    s.wait("sync", w3t[wi3 - 2])
                ld = s.dma("sync", wst3[:, sl:sl + 2048],
                           dr["fw1t"][128 * c:128 * (c + 1), 2048 * hf:2048 * (hf + 1)], "DX")
                s.wait("scalar", ld)
                w3t.append(act(f1q[:, H4 * c + 2048 * hf:H4 * c + 2048 * (hf + 1)],
                               wst3[:, sl:sl + 2048], AF.Copy))
                wi3 += 1
        for c in range(H4C):
            sl = 2048 * (wi3 % 2)
            if wi3 >= 2:
                s.wait("sync", w3t[wi3 - 2])
            ld = s.dma("sync", wst3[:, sl:sl + D], dr["fw2t"][128 * c:128 * (c + 1), :], "DX")
            s.wait("scalar", ld)
            w3t.append(act(f2q[:, D * c:D * (c + 1)], wst3[:, sl:sl + D], AF.Copy))
            wi3 += 1
        wrdy3 = s.tok("SA")
        for i in range(NT):
            t0 = i * TT
            if i > 0:
                s.wait("sync", s.tok("ST"))
                s.wait("scalar", s.tok("SV"))
            xqt = []
            for c in range(DC):
                xqt.append(s.dma("sync", xq8r[:, TT * c:TT * (c + 1)],
                                 xq8s[128 * c:128 * (c + 1), t0:t0 + TT], "DX"))
            s.wait("tensor", wrdy3, *xqt)
            hev = []
            for oc in range(H4C):
                pban = psA if oc % 2 == 0 else psB
                if oc >= 2:
                    s.wait("tensor", hev[oc - 2])
                for c in range(DC):
                    mm(pban[0:128, 0:TT], f1q[:, H4 * c + 128 * oc:H4 * c + 128 * (oc + 1)],
                       xq8r[:, TT * c:TT * (c + 1)], c == 0, c == DC - 1,
                       inc="ST" if c == DC - 1 else None)
                m1t = s.tok("ST")
                s.wait("scalar", m1t)
                act(h16t[:, TT * (oc % 2):TT * (oc % 2 + 1)], pban[0:128, 0:TT],
                    AF.Identity, bias=b1q[:, oc:oc + 1])
                hev.append(act(hq8[:, TT * oc:TT * (oc + 1)],
                               h16t[:, TT * (oc % 2):TT * (oc % 2 + 1)], AF.Relu))
            hq8_tok = s.tok("SA")
            ftk = None
            fhist = []
            ohist = []
            for sub in range(4):
                pp = sub % 2
                if sub >= 2:
                    s.wait("sync", fhist[sub - 2])
                ml = s.dma("sync", mk[:, 1024 * pp:1024 * (pp + 1)],
                           mask_d[t0 + 128 * sub:t0 + 128 * (sub + 1), :], "DX")
                pl = s.dma("sync", pbf[:, 1024 * pp:1024 * (pp + 1)],
                           out_d[t0 + 128 * sub:t0 + 128 * (sub + 1), :], "DX")
                s.wait("tensor", hq8_tok)
                if ftk is not None:
                    s.wait("tensor", ftk, yevp)
                for c in range(H4C):
                    mm(psW[0:128, 0:512],
                       hq8[:, TT * c + 128 * sub:TT * c + 128 * (sub + 1)],
                       f2q[:, D * c:D * c + 512], c == 0, c == H4C - 1)
                    mm(psW[0:128, 512:1024],
                       hq8[:, TT * c + 128 * sub:TT * c + 128 * (sub + 1)],
                       f2q[:, D * c + 512:D * (c + 1)], c == 0, c == H4C - 1,
                       inc="ST" if c == H4C - 1 else None)
                m2t = s.tok("ST")
                s.wait("scalar", m2t)
                if ftk is not None:
                    s.wait("scalar", ftk)
                yev = act(ytmp[:], psW[0:128, 0:1024], AF.Copy)
                s.wait("vector", yev, ml, pl)
                if sub >= 2:
                    s.wait("vector", ohist[sub - 2])
                vtt(y16[:], ytmp[:], b2tile[:], ALU.add)
                vtt(otb[:, 1024 * pp:1024 * (pp + 1)], y16[:],
                    mk[:, 1024 * pp:1024 * (pp + 1)], ALU.mult)
                ftk = vtt(otb[:, 1024 * pp:1024 * (pp + 1)],
                          otb[:, 1024 * pp:1024 * (pp + 1)],
                          pbf[:, 1024 * pp:1024 * (pp + 1)], ALU.add, inc="SV")
                fhist.append(ftk)
                yevp = yev
                s.wait("gpsimd", ftk)
                ohist.append(s.dma("gpsimd", out_d[t0 + 128 * sub:t0 + 128 * (sub + 1), :],
                                   otb[:, 1024 * pp:1024 * (pp + 1)], "DG"))
        ph3.close()

    s.wait("gpsimd", s.tok("DG"))
    s.wait("sync", s.tok("DX"))

    block = es.enter_context(nc.Block())
    s.realize(block)
    es.close()
    return nc


def _consts():
    cid = np.eye(128, dtype=BF)
    cmask = np.fromfunction(lambda r, c: ((r % 64) <= (c % 64)), (128, 128)).astype(BF)
    cob = np.fromfunction(lambda r, c: ((r // 64) == (c // 64)), (128, 128)).astype(BF)
    cones = np.ones((128, 1), BF)
    conesr = np.ones((1, 128), BF)
    cstart = np.zeros((128, TT), BF)
    cstart[:, ::C] = 1
    cbig = np.full((128, TT), 1e36, BF)
    cbig[:, ::C] = 0
    return cid, cmask, cob, cones, conesr, cstart, cbig


def kernel(L0_attn_x, L0_attn_Wq, L0_attn_Wk, L0_attn_Wv, L0_attn_Wg, L0_attn_Wo,
           L0_attn_w1, L0_attn_w2, L0_attn_w3, L0_attn_g_attn, L0_attn_g_ffn,
           L0_ffn_w1, L0_ffn_b1, L0_ffn_w2, L0_ffn_b2):
    x = np.asarray(L0_attn_x, np.float32)
    B = x.shape[0]
    maskml = _dropout_maskmul((B, T, D))
    cid, cmask, cob, cones, conesr, cstart, cbig = _consts()
    shared = {
        "wq": np.ascontiguousarray(L0_attn_Wq, dtype=np.float32),
        "wk": np.ascontiguousarray(L0_attn_Wk, dtype=np.float32),
        "wv": np.ascontiguousarray(L0_attn_Wv, dtype=np.float32),
        "wg": np.ascontiguousarray(L0_attn_Wg, dtype=np.float32),
        "wo": np.ascontiguousarray(L0_attn_Wo, dtype=np.float32),
        "w1a": np.ascontiguousarray(L0_attn_w1, dtype=np.float32),
        "w2a": np.ascontiguousarray(L0_attn_w2, dtype=np.float32),
        "w3a": np.ascontiguousarray(L0_attn_w3, dtype=np.float32),
        "fw1t": np.ascontiguousarray(np.asarray(L0_ffn_w1, np.float32).T),
        "fw2t": np.ascontiguousarray(np.asarray(L0_ffn_w2, np.float32).T),
        "ga": np.ascontiguousarray(np.asarray(L0_attn_g_attn, np.float32).reshape(DC, 128).T),
        "gf": np.ascontiguousarray(np.asarray(L0_attn_g_ffn, np.float32).reshape(DC, 128).T),
        "fb1": np.ascontiguousarray(np.asarray(L0_ffn_b1, np.float32).reshape(H4C, 128).T),
        "fb2": np.ascontiguousarray(np.asarray(L0_ffn_b2, np.float32).reshape(1, D)),
        "cid": cid, "cmask": cmask, "cob": cob, "cones": cones, "conesr": conesr,
        "cstart": cstart, "cbig": cbig, "conesrf": np.ones((1, 128), np.float32),
        "ceps": np.array([[1e-5, 1e-6]], np.float32),
    }
    nc = build_program()
    in_maps = []
    for b in range(B):
        m = dict(shared)
        m["xfm"] = np.ascontiguousarray(x[b].T)
        m["xtok"] = np.ascontiguousarray(x[b])
        m["maskml"] = np.ascontiguousarray(maskml[b])
        in_maps.append(m)
    res = run_bass_kernel_spmd(nc, in_maps, core_ids=list(range(B)))
    return np.stack([np.asarray(r["out"], np.float32) for r in res.results], axis=0)
